# version 13
# baseline (speedup 1.0000x reference)
import numpy as np
from contextlib import ExitStack

import concourse.bass as bass
import concourse.mybir as mybir
from concourse.bass_utils import run_bass_kernel_spmd

F32 = mybir.dt.float32
BF16 = mybir.dt.bfloat16
AF = mybir.ActivationFunctionType
ALU = mybir.AluOpType

NCORES = 8
D = 1024
L = 8192
HL = 4096
NS = 16
RK = 64
DH = 64
T1 = 512
NCH1 = L // T1
TA = 512
NCA = HL // TA
TB = 256
NCB = HL // TB
NPAIR = 4
GROUPS = [[0, 1], [2, 3], [4, 5], [6, 7]]

DEBUG_PHASE = 0

ENGS = ("t", "s", "v", "g", "q")


class P:

    def __init__(self, nc, stack):
        self.nc = nc
        self.e = {"t": nc.tensor, "s": nc.scalar, "v": nc.vector,
                  "g": nc.gpsimd, "q": nc.sync}
        names = ["t", "s", "v", "g", "ds", "dg", "da", "cc"]
        self.sem = {n: stack.enter_context(nc.semaphore("sem_" + n)) for n in names}
        self.cnt = {n: 0 for n in names}
        self.last_wait = {}

    def inc(self, inst, sem, n=1):
        inst.then_inc(self.sem[sem], n)
        self.cnt[sem] += n
        return self.cnt[sem]

    def wait(self, eng, sem, val=None):
        if val is None:
            val = self.cnt[sem]
        if val <= 0:
            return
        key = (eng, sem)
        if self.last_wait.get(key, -1) >= val:
            return
        self.e[eng].wait_ge(self.sem[sem], val)
        self.last_wait[key] = val

    def dma(self, queue, out, in_):
        smap = {"q": "ds", "g": "dg", "s": "da"}
        s = smap[queue]
        inst = self.e[queue].dma_start(out, in_)
        return self.inc(inst, s, 16)

    def barrier(self):
        snap = dict(self.cnt)
        for eng in ENGS:
            for s, v in snap.items():
                self.wait(eng, s, v)


def r8(sb_ap, r=8):
    return sb_ap.rearrange("p (r t) -> p r t", r=r)


def d8(dram_ap, p=128):
    return dram_ap.rearrange("(r p) t -> p r t", p=p)


def build_program(debug_phase=0):
    nc = bass.Bass("TRN2", target_bir_lowering=False)
    ap = lambda h: h.ap()

    xT = ap(nc.dram_tensor("xT", [D, L], F32, kind="ExternalInput"))
    x_mine = ap(nc.dram_tensor("x_mine", [D, HL], F32, kind="ExternalInput"))
    wxT = ap(nc.dram_tensor("wxT", [DH, 96], F32, kind="ExternalInput"))
    wx_nat = ap(nc.dram_tensor("wx_nat", [96, DH], F32, kind="ExternalInput"))
    wdtT = ap(nc.dram_tensor("wdtT", [RK, DH], F32, kind="ExternalInput"))
    a_log_arr = ap(nc.dram_tensor("a_log_arr", [128, NPAIR * NS], F32, kind="ExternalInput"))
    bdt_pair = ap(nc.dram_tensor("bdt_pair", [128, 1], F32, kind="ExternalInput"))
    woutT = ap(nc.dram_tensor("woutT", [D, D], F32, kind="ExternalInput"))
    bout_col = ap(nc.dram_tensor("bout_col", [128, 8], F32, kind="ExternalInput"))
    sgnT = ap(nc.dram_tensor("sgnT", [D, D], F32, kind="ExternalInput"))
    convw = ap(nc.dram_tensor("convw", [128, 24], F32, kind="ExternalInput"))
    convb_col = ap(nc.dram_tensor("convb_col", [128, 8], F32, kind="ExternalInput"))
    pwb_col = ap(nc.dram_tensor("pwb_col", [128, 8], F32, kind="ExternalInput"))
    conv_g = ap(nc.dram_tensor("conv_g", [128, 8], F32, kind="ExternalInput"))
    gru_g = ap(nc.dram_tensor("gru_g", [128, 8], F32, kind="ExternalInput"))
    ff_g = ap(nc.dram_tensor("ff_g", [128, 8], F32, kind="ExternalInput"))
    pwT = ap(nc.dram_tensor("pwT", [D, D], F32, kind="ExternalInput"))
    gruT = ap(nc.dram_tensor("gruT", [D, 2 * D], F32, kind="ExternalInput"))
    ff1T = ap(nc.dram_tensor("ff1T", [D, 4 * D], F32, kind="ExternalInput"))
    ff2T = ap(nc.dram_tensor("ff2T", [4 * D, D], F32, kind="ExternalInput"))
    b1_col = ap(nc.dram_tensor("b1_col", [128, 32], F32, kind="ExternalInput"))
    b2_col = ap(nc.dram_tensor("b2_col", [128, 8], F32, kind="ExternalInput"))
    maskA = ap(nc.dram_tensor("maskA", [128, 1], F32, kind="ExternalInput"))
    maskB = ap(nc.dram_tensor("maskB", [128, 1], F32, kind="ExternalInput"))
    mrecv = ap(nc.dram_tensor("mrecv", [128, 1], F32, kind="ExternalInput"))

    yout = ap(nc.dram_tensor("yout", [D, HL], F32, kind="ExternalOutput"))
    taps = {}
    if debug_phase == 11:
        taps["ln1f"] = ap(nc.dram_tensor("tap_ln1f", [128, 4 * T1], F32, kind="ExternalOutput"))
        taps["dt"] = ap(nc.dram_tensor("tap_dt", [128, T1], F32, kind="ExternalOutput"))
        taps["dtx"] = ap(nc.dram_tensor("tap_dtx", [128, T1], F32, kind="ExternalOutput"))
        taps["hbuf"] = ap(nc.dram_tensor("tap_hbuf", [128, NS * T1], F32, kind="ExternalOutput"))
        taps["abig"] = ap(nc.dram_tensor("tap_abig", [128, NS * T1], F32, kind="ExternalOutput"))
        taps["stb"] = ap(nc.dram_tensor("tap_stb", [128, 256], F32, kind="ExternalOutput"))
        taps["bexp0"] = ap(nc.dram_tensor("tap_bexp0", [128, T1], F32, kind="ExternalOutput"))
        taps["ubig"] = ap(nc.dram_tensor("tap_ubig", [128, NS * T1], F32, kind="ExternalOutput"))
        taps["y0"] = ap(nc.dram_tensor("tap_y0", [128, T1], F32, kind="ExternalOutput"))
        taps["aarr"] = ap(nc.dram_tensor("tap_aarr", [128, NPAIR * NS], F32, kind="ExternalOutput"))
        taps["stdt"] = ap(nc.dram_tensor("tap_stdt", [128, 128], F32, kind="ExternalOutput"))

    y_stage = nc.dram_tensor("y_stage", [512, L], BF16)
    y_gath = nc.dram_tensor("y_gath", [1024, L], BF16)
    halo_in = nc.dram_tensor("halo_in", [128, 16], F32)
    halo_out = nc.dram_tensor("halo_out", [256, 16], F32)
    gst_in = nc.dram_tensor("gst_in", [128, 8], F32)
    gst_out = nc.dram_tensor("gst_out", [256, 8], F32)
    x2_save = nc.dram_tensor("x2_save", [D, HL], F32)
    z_save = nc.dram_tensor("z_save", [D, HL], F32)
    p_save = nc.dram_tensor("p_save", [D, HL], BF16)
    ff1_bf = nc.dram_tensor("ff1_bf", [D, 4 * D], BF16)
    ff2_bf = nc.dram_tensor("ff2_bf", [4 * D, D], BF16)

    outer = ExitStack()
    with outer:
        p = P(nc, outer)
        sb = lambda name, shape, dt=F32: outer.enter_context(nc.sbuf_tensor(name, shape, dt))
        psum = lambda name: outer.enter_context(nc.psum_tensor(name, [128, 512], F32))

        ones_col = sb("ones_col", [128, 1])
        ones_row = sb("ones_row", [1, 128])
        m_A = sb("m_A", [128, 1]); m_B = sb("m_B", [128, 1]); m_R = sb("m_R", [128, 1])
        a_arr = sb("a_arr", [128, NPAIR * NS])
        bdt_t = sb("bdt_t", [128, 1])
        bout_t = sb("bout_t", [128, 8])
        cgam = sb("cgam", [128, 8]); ggam = sb("ggam", [128, 8]); fgam = sb("fgam", [128, 8])
        convw_t = sb("convw_t", [128, 24])
        convb_t = sb("convb_t", [128, 8]); pwb_t = sb("pwb_t", [128, 8])
        beff_t = sb("beff_t", [128, 8])
        b1_t = sb("b1_t", [128, 32]); b2_t = sb("b2_t", [128, 8])
        ssm_state = sb("ssm_state", [128, NPAIR * NS])
        gru_state = sb("gru_state", [128, 8])
        p_state = sb("p_state", [128, 8])
        gst_sb = sb("gst_sb", [128, 8])
        row = sb("row", [1, 4 * 512])
        cst = sb("cst", [128, 4])
        tmpw = sb("tmpw", [128, D])

        ps_s1 = psum("ps_s1"); ps_s2 = psum("ps_s2")
        ps_r = psum("ps_r"); ps_m = psum("ps_m")
        ps_a = psum("ps_a"); ps_b = psum("ps_b")
        ps_c = psum("ps_c"); ps_d = psum("ps_d")

        p.inc(nc.vector.memset(cst[:, 0:1], 1e-5), "v")
        p.inc(nc.vector.memset(cst[:, 1:2], -0.5), "v")
        p.inc(nc.vector.memset(cst[:, 2:3], 1.0 / D), "v")
        p.inc(nc.vector.memset(cst[:, 3:4], 32.0), "v")
        p.dma("q", m_A[:], maskA); p.dma("q", m_B[:], maskB); p.dma("q", m_R[:], mrecv)
        p.dma("q", bdt_t[:], bdt_pair); p.dma("q", bout_t[:], bout_col)
        p.dma("q", convw_t[:], convw); p.dma("q", convb_t[:], convb_col)
        p.dma("q", pwb_t[:], pwb_col)
        p.dma("q", b1_t[:], b1_col); p.dma("q", b2_t[:], b2_col)
        tmp_al = sb("tmp_al", [128, NPAIR * NS])
        tmp_g = sb("tmp_g", [128, 24])
        p.dma("q", tmp_al[:], a_log_arr)
        p.dma("q", tmp_g[:, 0:8], conv_g); p.dma("q", tmp_g[:, 8:16], gru_g)
        p.dma("q", tmp_g[:, 16:24], ff_g)
        wx_t = sb("wx_t", [96, DH]); wdt_t = sb("wdt_t", [RK, DH])
        p.dma("q", wx_t[:], wx_nat); p.dma("q", wdt_t[:], wdtT)

        p.wait("s", "ds"); p.wait("s", "dg"); p.wait("s", "v")
        p.inc(nc.scalar.activation(a_arr[:], tmp_al[:], AF.Exp), "s")
        p.inc(nc.scalar.mul(a_arr[:], a_arr[:], -1.0), "s")
        p.wait("v", "ds")
        p.inc(nc.vector.tensor_scalar(cgam[:], tmp_g[:, 0:8], 32.0, 32.0,
                                      ALU.mult, ALU.add), "v")
        p.inc(nc.vector.tensor_scalar(ggam[:], tmp_g[:, 8:16], 32.0, 32.0,
                                      ALU.mult, ALU.add), "v")
        p.inc(nc.vector.tensor_scalar(fgam[:], tmp_g[:, 16:24], 32.0, 32.0,
                                      ALU.mult, ALU.add), "v")
        p.inc(nc.vector.memset(ssm_state[:], 0.0), "v")
        p.inc(nc.vector.memset(gru_state[:], 0.0), "v")
        p.inc(nc.vector.memset(p_state[:], 1.0), "v")
        p.inc(nc.vector.memset(ones_col[:], 1.0), "v")
        p.inc(nc.vector.memset(ones_row[:], 1.0), "v")

        p.barrier()

        def stats(src_slices, T, rms_only):
            sqb = stats.sqb
            p.wait("s", "v")
            if not rms_only:
                for r in range(8):
                    p.inc(nc.tensor.matmul(ps_s1[0:1, 0:T], ones_col[:], src_slices[r],
                                           start=(r == 0), stop=(r == 7)), "t")
            for r in range(8):
                sq = sqb[r % 2]
                if stats.tok_sq[r % 2]:
                    p.wait("s", "t", stats.tok_sq[r % 2])
                p.inc(nc.scalar.activation(sq[:, 0:T], src_slices[r], AF.Square), "s")
                p.wait("t", "s")
                stats.tok_sq[r % 2] = p.inc(
                    nc.tensor.matmul(ps_s2[0:1, 0:T], ones_col[:], sq[:, 0:T],
                                     start=(r == 0), stop=(r == 7)), "t")
            if rms_only:
                p.wait("v", "t")
                p.inc(nc.vector.tensor_scalar_max(row[0:1, 0:T], ps_s2[0:1, 0:T], 1e-24), "v")
                p.wait("s", "v")
                p.inc(nc.scalar.activation(row[0:1, 512:512 + T], row[0:1, 0:T], AF.Ln), "s")
                p.inc(nc.scalar.activation(row[0:1, 512:512 + T], row[0:1, 512:512 + T],
                                           AF.Exp, scale=cst[0:1, 1:2]), "s")
                p.wait("t", "s")
                p.inc(nc.tensor.matmul(ps_r[:, 0:T], ones_row[0:1, :], row[0:1, 512:512 + T],
                                       start=True, stop=True), "t")
                return
            p.wait("s", "t")
            p.inc(nc.scalar.activation(row[0:1, 0:T], ps_s1[0:1, 0:T], AF.Square,
                                       scale=cst[0:1, 2:3]), "s")
            p.wait("v", "s")
            p.inc(nc.vector.scalar_tensor_tensor(row[0:1, 512:512 + T], ps_s2[0:1, 0:T],
                                                 1.0 / D, row[0:1, 0:T],
                                                 ALU.mult, ALU.subtract), "v")
            p.wait("s", "v")
            p.inc(nc.scalar.activation(row[0:1, 1024:1024 + T], row[0:1, 512:512 + T],
                                       AF.Ln, bias=cst[0:1, 0:1]), "s")
            p.inc(nc.scalar.activation(row[0:1, 1024:1024 + T], row[0:1, 1024:1024 + T],
                                       AF.Exp, scale=cst[0:1, 1:2]), "s")
            p.wait("v", "s")
            p.inc(nc.vector.scalar_tensor_tensor(row[0:1, 1536:1536 + T], ps_s1[0:1, 0:T],
                                                 -1.0 / D, row[0:1, 1024:1024 + T],
                                                 ALU.mult, ALU.mult), "v")
            p.wait("t", "v")
            p.inc(nc.tensor.matmul(ps_r[:, 0:T], ones_row[0:1, :], row[0:1, 1024:1024 + T],
                                   start=True, stop=True), "t")
            p.inc(nc.tensor.matmul(ps_m[:, 0:T], ones_row[0:1, :], row[0:1, 1536:1536 + T],
                                   start=True, stop=True), "t")

        stats.sqb = [sb("sqs0", [128, 512]), sb("sqs1", [128, 512])]
        stats.tok_sq = [0, 0]
        stats.cons = {"s": 0, "v": 0}

        def stats_mark():
            stats.cons = {"s": p.cnt["s"], "v": p.cnt["v"]}

        def stats_guard():
            p.wait("t", "s", stats.cons["s"])
            p.wait("t", "v", stats.cons["v"])

        ph_st = ExitStack()
        sbst = lambda name, shape, dt=F32: ph_st.enter_context(nc.sbuf_tensor(name, shape, dt))
        st_dt = sbst("st_dt", [128, 128], BF16)
        st_B = sbst("st_B", [128, 128], BF16)
        st_C = sbst("st_C", [128, 128], BF16)
        wx_t2 = sbst("wx_t2", [128, 96])
        p.dma("q", wx_t2[0:64, :], wxT)
        p.dma("q", wx_t2[64:128, :], wxT)
        p.wait("t", "ds")
        p.inc(nc.tensor.matmul(ps_a[0:DH, 0:DH], wx_t[0:RK, 0:DH], wdt_t[:],
                               start=True, stop=True), "t")
        p.inc(nc.vector.memset(st_dt[:], 0.0), "v")
        p.inc(nc.vector.memset(st_B[:], 0.0), "v")
        p.inc(nc.vector.memset(st_C[:], 0.0), "v")
        p.wait("v", "t")
        p.inc(nc.vector.tensor_copy(st_dt[0:DH, 0:DH], ps_a[0:DH, 0:DH]), "v")
        p.inc(nc.vector.tensor_copy(st_dt[DH:128, DH:128], ps_a[0:DH, 0:DH]), "v")
        p.wait("v", "ds")
        for (st, c0) in ((st_B, RK), (st_C, RK + NS)):
            p.inc(nc.vector.tensor_copy(st[0:DH, 0:NS], wx_t2[0:DH, c0:c0 + NS]), "v")
            p.inc(nc.vector.tensor_copy(st[0:DH, 2 * NS:3 * NS], wx_t2[0:DH, c0:c0 + NS]), "v")
            p.inc(nc.vector.tensor_copy(st[DH:128, 4 * NS:5 * NS],
                                        wx_t2[DH:128, c0:c0 + NS]), "v")
            p.inc(nc.vector.tensor_copy(st[DH:128, 6 * NS:7 * NS],
                                        wx_t2[DH:128, c0:c0 + NS]), "v")

        if debug_phase == 11:
            stdt_f = sbst("stdt_f", [128, 128])
            p.wait("s", "v")
            p.inc(nc.scalar.copy(stdt_f[:], st_dt[:]), "s")
            p.wait("q", "s")
            p.dma("q", taps["stdt"], stdt_f[:])
        ph1 = ExitStack()
        with ph1:
            s1 = lambda name, shape, dt=F32: ph1.enter_context(nc.sbuf_tensor(name, shape, dt))
            xc0 = s1("xc0", [128, 8 * T1])
            xc = [xc0, xc0]
            ln1f = [s1(f"ln1f{i}", [128, 4 * T1]) for i in range(2)]
            ln1b = [s1(f"ln1b{i}", [128, 4 * T1], BF16) for i in range(2)]
            dt_f = s1("dt_f", [128, T1]); dtx_f = s1("dtx_f", [128, T1], BF16)
            Barr = s1("Barr", [128, T1], BF16); Carr = s1("Carr", [128, T1], BF16)
            bexp = s1("bexp", [128, T1], BF16); cexp = s1("cexp", [128, T1], BF16)
            sp1 = s1("sp1", [128, T1]); sp2 = s1("sp2", [128, T1])
            abig = s1("abig", [128, NS * T1])
            ubuf_all = s1("ubuf_all", [128, NS * T1])
            hbuf = s1("hbuf", [128, NS * T1])
            hcbuf = [s1(f"hcbuf{i}", [128, NS * T1], BF16) for i in range(2)]
            ybf = [s1(f"ybf{k}", [128, T1], BF16) for k in range(NPAIR)]

            exp_tok = [0] * NS
            abig_war_tok = [0]
            barr_cons_tok = [0]
            tree_tok = [0, 0]
            last_state_tok = 0
            tok_xc = [(0, 0, 0)] * 2
            tok_ydma = [0] * NPAIR
            prev_sp_tok = [0]
            ld_tok = [0, 0]

            ld_tok[0] = p.dma("q", r8(xc[0][:]), d8(xT[:, 0:T1]))
            stok_lnb_c = [0, 0]

            def ln1_chunk(ci):
                cur = ci % 2
                X = xc[cur]
                Xs = [X[:, r * T1:(r + 1) * T1] for r in range(8)]
                stats_guard()
                p.wait("t", "ds", ld_tok[cur])
                p.wait("s", "ds", ld_tok[cur])
                stats(Xs, T1, rms_only=False)
                p.wait("v", "t")
                for k in range(NPAIR):
                    sl = slice(k * T1, (k + 1) * T1)
                    p.inc(nc.vector.tensor_mul(ln1f[cur][:, sl], Xs[k],
                                               ps_r[:, 0:T1]), "v")
                    p.inc(nc.vector.tensor_add(ln1f[cur][:, sl], ln1f[cur][:, sl],
                                               ps_m[:, 0:T1]), "v")
                vtok_ln = p.cnt["v"]
                tok_xc[cur] = (p.cnt["t"], p.cnt["s"], vtok_ln)
                p.wait("s", "v", vtok_ln)
                p.wait("s", "t")
                for k in range(NPAIR):
                    sl = slice(k * T1, (k + 1) * T1)
                    p.inc(nc.scalar.copy(ln1b[cur][:, sl], ln1f[cur][:, sl]), "s")
                stok_lnb_c[cur] = p.cnt["s"]
                stats_mark()

            ln1_chunk(0)
            for ci in range(NCH1):
                cur = ci % 2
                nxt = (ci + 1) % 2
                if ci + 1 < NCH1:
                    for par in (0, 1):
                        tt, ts, tv = tok_xc[par]
                        p.wait("q", "t", tt); p.wait("q", "s", ts); p.wait("q", "v", tv)
                    ld_tok[nxt] = p.dma("q", r8(xc[nxt][:]),
                                        d8(xT[:, T1 * (ci + 1):T1 * (ci + 2)]))
                stok_lnb = stok_lnb_c[cur]

                for k in range(NPAIR):
                    if k == 1 and ci + 1 < NCH1:
                        ln1_chunk(ci + 1)
                    sl = slice(k * T1, (k + 1) * T1)
                    p.wait("t", "s", stok_lnb)
                    p.wait("t", "s", prev_sp_tok[0])
                    p.inc(nc.tensor.matmul(ps_a[:, 0:T1], st_dt[:], ln1b[cur][:, sl],
                                           start=True, stop=True), "t")
                    p.inc(nc.tensor.matmul(ps_b[:, 0:T1], st_B[:], ln1b[cur][:, sl],
                                           start=True, stop=True), "t")
                    p.inc(nc.tensor.matmul(ps_c[:, 0:T1], st_C[:], ln1b[cur][:, sl],
                                           start=True, stop=True), "t")
                    t_dummy = p.inc(nc.tensor.matmul(ps_d[:, 0:T1], st_dt[:],
                                                     ln1b[cur][:, sl],
                                                     start=True, stop=True), "t")
                    p.wait("s", "t", t_dummy)
                    p.wait("s", "v", barr_cons_tok[0])
                    p.inc(nc.scalar.activation(sp1[:], ps_a[:, 0:T1], AF.Abs,
                                               bias=bdt_t[:, 0:1]), "s")
                    p.inc(nc.scalar.activation(sp1[:], sp1[:], AF.Exp, scale=-1.0), "s")
                    p.inc(nc.scalar.activation(sp1[:], sp1[:], AF.Ln, bias=1.0), "s")
                    p.inc(nc.scalar.activation(sp2[:], ps_a[:, 0:T1], AF.Relu,
                                               bias=bdt_t[:, 0:1]), "s")
                    p.inc(nc.scalar.copy(Barr[:], ps_b[:, 0:T1]), "s")
                    p.inc(nc.scalar.copy(Carr[:], ps_c[:, 0:T1]), "s")
                    stok_sp = p.cnt["s"]
                    stok_bc = stok_sp
                    prev_sp_tok[0] = stok_sp
                    p.wait("v", "s", stok_sp)
                    p.inc(nc.vector.tensor_add(dt_f[:], sp1[:], sp2[:]), "v")
                    p.inc(nc.vector.tensor_mul(dtx_f[:], dt_f[:], ln1f[cur][:, sl]), "v")
                    vtok_dtx = p.cnt["v"]
                    p.wait("s", "v", abig_war_tok[0])
                    p.wait("s", "v", vtok_dtx)
                    for n in range(NS):
                        p.inc(nc.scalar.activation(
                            abig[:, n * T1:(n + 1) * T1], dt_f[:], AF.Exp,
                            scale=a_arr[:, k * NS + n:k * NS + n + 1]), "s")
                        exp_tok[n] = p.cnt["s"]
                    hcb = hcbuf[k % 2]
                    p.wait("v", "g", tree_tok[k % 2])
                    U32 = mybir.dt.uint32
                    bc = lambda apx: apx.bitcast(U32)
                    p.wait("v", "s", stok_bc)
                    for n in range(NS):
                        p.inc(nc.vector.stream_shuffle(bc(bexp[:]), bc(Barr[:]),
                                                       [n] * 32), "v")
                        p.inc(nc.vector.tensor_mul(ubuf_all[:, n * T1:(n + 1) * T1],
                                                   bexp[:], dtx_f[:]), "v")
                    p.wait("v", "s", exp_tok[NS - 1])
                    for n in range(NS):
                        c = n * T1
                        p.inc(nc.vector.tensor_tensor_scan(
                            hbuf[:, c:c + T1], abig[:, c:c + T1],
                            ubuf_all[:, c:c + T1],
                            ssm_state[:, k * NS + n:k * NS + n + 1],
                            ALU.mult, ALU.add), "v")
                    scan_tok = p.cnt["v"]
                    abig_war_tok[0] = scan_tok
                    for n in range(NS):
                        p.inc(nc.vector.stream_shuffle(bc(cexp[:]), bc(Carr[:]),
                                                       [n] * 32), "v")
                        p.inc(nc.vector.tensor_mul(hcb[:, n * T1:(n + 1) * T1],
                                                   hbuf[:, n * T1:(n + 1) * T1],
                                                   cexp[:]), "v")
                    barr_cons_tok[0] = p.cnt["v"]
                    if debug_phase == 11 and ci == 0 and k == 0:
                        p.wait("q", "v"); p.wait("q", "s")
                        p.wait("g", "v"); p.wait("g", "s")
                        p.dma("q", taps["ln1f"], ln1f[cur][:])
                        p.dma("q", taps["dt"], dt_f[:])
                        p.dma("g", taps["dtx"], dtx_f[:])
                        p.dma("g", taps["hbuf"], hbuf[:])
                        p.dma("q", taps["aarr"], a_arr[:])
                        p.wait("v", "ds"); p.wait("s", "ds"); p.wait("g", "ds")
                    p.wait("s", "v", scan_tok)
                    hview = hbuf[:].rearrange("p (n t) -> p n t", n=NS)
                    p.inc(nc.scalar.copy(ssm_state[:, k * NS:(k + 1) * NS],
                                         hview[:, :, T1 - 1:T1]), "s")
                    stok_state = p.cnt["s"]
                    last_state_tok = stok_state
                    p.wait("g", "v")
                    w = NS * T1
                    while w > T1:
                        h = w // 2
                        p.inc(nc.gpsimd.tensor_add(hcb[:, 0:h], hcb[:, 0:h],
                                                   hcb[:, h:w]), "g")
                        w = h
                    if tok_ydma[k]:
                        p.wait("g", "ds", tok_ydma[k])
                    p.inc(nc.gpsimd.tensor_copy(ybf[k][:], hcb[:, 0:T1]), "g")
                    tree_tok[k % 2] = p.cnt["g"]
                    if debug_phase == 11 and ci == 0 and k == 0:
                        p.wait("q", "g", tree_tok[k % 2])
                        p.dma("g", taps["y0"], hcb[:, 0:T1])
                        p.wait("v", "dg"); p.wait("s", "dg"); p.wait("g", "dg")
                    p.wait("q", "g", tree_tok[k % 2])
                    tok_ydma[k] = p.dma(
                        "q", y_stage.ap()[128 * k:128 * (k + 1), T1 * ci:T1 * (ci + 1)],
                        ybf[k][:])

            p.wait("g", "ds")
            p.inc(nc.gpsimd.collective_compute(
                "AllGather", ALU.bypass, replica_groups=GROUPS,
                ins=[y_stage.ap()], outs=[y_gath.ap()]), "cc")

        open_stacks = []
        ph_res = ExitStack(); open_stacks.append(ph_res)
        ph_st.close()
        xres = [ph_res.enter_context(nc.sbuf_tensor(f"xres{r}", [128, HL], F32))
                for r in range(8)]
        ph_w = ExitStack(); open_stacks.append(ph_w)
        wout_b = [ph_w.enter_context(nc.sbuf_tensor(f"wout_b{r}", [128, D], BF16))
                  for r in range(8)]
        for r in range(8):
            p.dma("g", wout_b[r][:], woutT[128 * r:128 * (r + 1), :])
        p.barrier()

        ph1b = ExitStack()
        with ph1b:
            s1b = lambda name, shape, dt=F32: ph1b.enter_context(nc.sbuf_tensor(name, shape, dt))
            yA = s1b("yA", [128, 8 * T1], BF16)
            yB = s1b("yB", [128, 8 * T1], BF16)
            ym = s1b("ym", [128, 8 * T1], BF16)
            xm = s1b("xm", [128, 8 * T1])
            tok_1b = [0, 0]
            p.wait("q", "cc")
            for ci in range(8):
                p.wait("q", "v")
                tA = p.dma("q", r8(yA[:]), d8(y_gath.ap()[:, T1 * ci:T1 * (ci + 1)]))
                tB = p.dma("q", r8(yB[:]), d8(y_gath.ap()[:, HL + T1 * ci:HL + T1 * (ci + 1)]))
                tX = p.dma("q", r8(xm[:]), d8(x_mine[:, T1 * ci:T1 * (ci + 1)]))
                p.wait("v", "ds", tB)
                p.wait("v", "t")
                for r in range(8):
                    sl = slice(r * T1, (r + 1) * T1)
                    p.inc(nc.vector.tensor_scalar(ym[:, sl], yB[:, sl], m_B[:, 0:1], None,
                                                  ALU.mult), "v")
                    p.inc(nc.vector.scalar_tensor_tensor(ym[:, sl], yA[:, sl], m_A[:, 0:1],
                                                         ym[:, sl], ALU.mult, ALU.add), "v")
                vtok_ym = p.cnt["v"]
                p.wait("t", "v", vtok_ym)
                p.wait("t", "dg")
                for m in range(8):
                    pbank = ps_a if m % 2 == 0 else ps_b
                    p.wait("t", "v", tok_1b[m % 2])
                    for kk in range(8):
                        p.inc(nc.tensor.matmul(pbank[:, 0:T1],
                                               wout_b[kk][:, 128 * m:128 * (m + 1)],
                                               ym[:, kk * T1:(kk + 1) * T1],
                                               start=(kk == 0), stop=(kk == 7)), "t")
                    p.wait("v", "t")
                    p.wait("v", "ds", tX)
                    p.inc(nc.vector.scalar_tensor_tensor(
                        xres[m][:, T1 * ci:T1 * (ci + 1)], pbank[:, 0:T1],
                        bout_t[:, m:m + 1], xm[:, m * T1:(m + 1) * T1],
                        ALU.add, ALU.add), "v")
                    tok_1b[m % 2] = p.cnt["v"]
        p.barrier()
        ph_w.close(); open_stacks.remove(ph_w)

        def emit_debug_and_return():
            for r in range(8):
                p.wait("q", "v"); p.wait("q", "s"); p.wait("q", "t")
                p.dma("q", yout[128 * r:128 * (r + 1), :], xres[r][:])
            p.wait("q", "ds"); p.wait("q", "da"); p.wait("q", "dg")
            for st_ in reversed(open_stacks):
                st_.close()

        if debug_phase in (1, 11):
            emit_debug_and_return()
            return nc

        def load_sign(tmp, sgn_b):
            for r in range(8):
                t = p.dma("q", tmp[:], sgnT[128 * r:128 * (r + 1), :])
                p.wait("s", "ds", t)
                p.inc(nc.scalar.activation(sgn_b[r][:], tmp[:], AF.Sign), "s")
                p.wait("q", "s")

        def ln_apply_bf16(src_slices, T, dst, dst_T=None):
            dst_T = dst_T or T
            p.wait("v", "t")
            for r in range(8):
                p.inc(nc.vector.tensor_mul(stats.sqb[0][:, 0:T], src_slices[r],
                                           ps_r[:, 0:T]), "v")
                p.inc(nc.vector.tensor_add(dst[:, r * dst_T:r * dst_T + T],
                                           stats.sqb[0][:, 0:T], ps_m[:, 0:T]), "v")
            stats_mark()

        def sign_ffn(src_tiles, ci, T, add_into, sgn_b, lnb, out_dma=None):
            c0 = T * ci
            src = [src_tiles[r][:, c0:c0 + T] for r in range(8)]
            stats_guard()
            stats(src, T, rms_only=False)
            ln_apply_bf16(src, T, lnb, dst_T=TA)
            vtok = p.cnt["v"]
            p.wait("t", "v", vtok)
            for m in range(8):
                pbank = ps_a if m % 2 == 0 else ps_b
                p.wait("t", "v", sign_ffn.tok[m % 2])
                for kk in range(8):
                    p.inc(nc.tensor.matmul(pbank[:, 0:T],
                                           sgn_b[kk][:, 128 * m:128 * (m + 1)],
                                           lnb[:, kk * TA:kk * TA + T],
                                           start=(kk == 0), stop=(kk == 7)), "t")
                p.wait("v", "t")
                p.inc(nc.vector.tensor_add(add_into[m][:, c0:c0 + T],
                                           add_into[m][:, c0:c0 + T], pbank[:, 0:T]), "v")
                sign_ffn.tok[m % 2] = p.cnt["v"]
                if out_dma is not None:
                    p.wait("q", "v")
                    p.dma("q", out_dma[128 * m:128 * (m + 1), c0:c0 + T],
                          add_into[m][:, c0:c0 + T])

        sign_ffn.tok = [0, 0]
        ph2s = ExitStack(); open_stacks.append(ph2s)
        sgn_b = [ph2s.enter_context(nc.sbuf_tensor(f"sgn_b{r}", [128, D], BF16))
                 for r in range(8)]
        lnb = ph2s.enter_context(nc.sbuf_tensor("lnb", [128, 8 * TA], BF16))
        load_sign(tmpw, sgn_b)
        p.barrier()
        for ci in range(NCA):
            sign_ffn(xres, ci, TA, xres, sgn_b, lnb)
        p.barrier()
        ph2s.close(); open_stacks.remove(ph2s)
        if debug_phase == 2:
            emit_debug_and_return()
            return nc

        ph_ffb = ExitStack(); open_stacks.append(ph_ffb)
        ffb = ph_ffb.enter_context(nc.sbuf_tensor("ffb", [128, 4 * D], BF16))
        for r in range(8):
            p.wait("g", "ds")
            t = p.dma("g", ffb[:], ff1T[128 * r:128 * (r + 1), :])
            p.wait("q", "dg", t)
            p.dma("q", ff1_bf.ap()[128 * r:128 * (r + 1), :], ffb[:])
        for r in range(8):
            p.wait("g", "ds")
            t = p.dma("g", r8(ffb[:, 0:4 * D], r=4),
                      ff2T[512 * r:512 * (r + 1), :].rearrange("(r p) t -> p r t", p=128))
            p.wait("q", "dg", t)
            p.dma("q", ff2_bf.ap()[512 * r:512 * (r + 1), :]
                  .rearrange("(r p) t -> p r t", p=128), r8(ffb[:, 0:4 * D], r=4))
        p.wait("q", "dg"); p.wait("q", "ds")
        p.barrier()
        ph_ffb.close(); open_stacks.remove(ph_ffb)

        for ci in range(NCA):
            c0 = TA * ci
            src = [xres[r][:, c0:c0 + TA] for r in range(8)]
            stats_guard()
            stats(src, TA, rms_only=False)
            for r in range(8):
                p.dma("q", x2_save.ap()[128 * r:128 * (r + 1), c0:c0 + TA], src[r])
            p.wait("v", "t")
            p.wait("v", "ds")
            for r in range(8):
                p.inc(nc.vector.tensor_mul(stats.sqb[0][:], src[r], ps_r[:, 0:TA]), "v")
                p.inc(nc.vector.tensor_add(src[r], stats.sqb[0][:], ps_m[:, 0:TA]), "v")
            stats_mark()
        p.barrier()
        if debug_phase == 25:
            emit_debug_and_return()
            return nc

        ph3a = ExitStack()
        with ph3a:
            s3 = lambda name, shape, dt=F32: ph3a.enter_context(nc.sbuf_tensor(name, shape, dt))
            pw_b = [s3(f"pw_b{r}", [128, D], BF16) for r in range(8)]
            for r in range(8):
                p.dma("g", pw_b[r][:], pwT[128 * r:128 * (r + 1), :])
            xnf = s3("xnf", [128, 8 * (TA + 2)])
            cvb = s3("cvb", [128, 8 * TA], BF16)
            halo_sb = s3("halo_sb", [128, 16])
            cvt = s3("cvt", [128, TA])
            tok_pw = [0, 0]

            stats_guard()
            stats([xres[r][:, HL - 2:HL] for r in range(8)], 2, rms_only=True)
            p.wait("v", "t")
            for r in range(8):
                p.inc(nc.vector.scalar_tensor_tensor(halo_sb[:, 2 * r:2 * r + 2],
                                                     xres[r][:, HL - 2:HL],
                                                     cgam[:, r:r + 1], ps_r[:, 0:2],
                                                     ALU.mult, ALU.mult), "v")
            stats_mark()
            p.wait("q", "v")
            p.dma("q", halo_in.ap(), halo_sb[:])
            p.wait("g", "ds")
            p.inc(nc.gpsimd.collective_compute(
                "AllGather", ALU.bypass, replica_groups=GROUPS,
                ins=[halo_in.ap()], outs=[halo_out.ap()]), "cc")
            p.wait("q", "cc")
            tH = p.dma("q", halo_sb[:], halo_out.ap()[0:128, :])
            p.wait("v", "ds", tH)
            for r in range(8):
                base = r * (TA + 2)
                p.inc(nc.vector.tensor_scalar(xnf[:, base:base + 2],
                                              halo_sb[:, 2 * r:2 * r + 2],
                                              m_R[:, 0:1], None, ALU.mult), "v")

            convb_bf = s3("convb_bf", [128, 8], BF16)
            p.inc(nc.vector.tensor_copy(convb_bf[:], convb_t[:]), "v")
            p.wait("t", "dg")
            p.wait("t", "v")
            for m in range(8):
                for kk in range(8):
                    p.inc(nc.tensor.matmul(ps_d[:, m:m + 1],
                                           pw_b[kk][:, 128 * m:128 * (m + 1)],
                                           convb_bf[:, kk:kk + 1],
                                           start=(kk == 0), stop=(kk == 7)), "t")
            p.wait("v", "t")
            p.inc(nc.vector.tensor_add(beff_t[:], ps_d[:, 0:8], pwb_t[:]), "v")

            for ci in range(NCA):
                c0 = TA * ci
                stats_guard()
                stats([xres[r][:, c0:c0 + TA] for r in range(8)], TA, rms_only=True)
                p.wait("v", "t")
                for r in range(8):
                    base = r * (TA + 2)
                    p.inc(nc.vector.scalar_tensor_tensor(
                        xnf[:, base + 2:base + 2 + TA], xres[r][:, c0:c0 + TA],
                        cgam[:, r:r + 1], ps_r[:, 0:TA], ALU.mult, ALU.mult), "v")
                stats_mark()
                for r in range(8):
                    base = r * (TA + 2)
                    p.inc(nc.vector.tensor_scalar(cvt[:], xnf[:, base:base + TA],
                                                  convw_t[:, 3 * r:3 * r + 1], None,
                                                  ALU.mult), "v")
                    p.inc(nc.vector.scalar_tensor_tensor(cvt[:], xnf[:, base + 1:base + 1 + TA],
                                                         convw_t[:, 3 * r + 1:3 * r + 2],
                                                         cvt[:], ALU.mult, ALU.add), "v")
                    p.inc(nc.vector.scalar_tensor_tensor(cvb[:, r * TA:(r + 1) * TA],
                                                         xnf[:, base + 2:base + 2 + TA],
                                                         convw_t[:, 3 * r + 2:3 * r + 3],
                                                         cvt[:], ALU.mult, ALU.add), "v")
                    p.inc(nc.vector.tensor_copy(xnf[:, base:base + 2],
                                                xnf[:, base + TA:base + TA + 2]), "v")
                vtok_cv = p.cnt["v"]
                p.wait("t", "v", vtok_cv)
                for m in range(8):
                    pbank = ps_a if m % 2 == 0 else ps_b
                    p.wait("t", "v", tok_pw[m % 2])
                    for kk in range(8):
                        p.inc(nc.tensor.matmul(pbank[:, 0:TA],
                                               pw_b[kk][:, 128 * m:128 * (m + 1)],
                                               cvb[:, kk * TA:(kk + 1) * TA],
                                               start=(kk == 0), stop=(kk == 7)), "t")
                    p.wait("v", "t")
                    p.inc(nc.vector.scalar_tensor_tensor(
                        xres[m][:, c0:c0 + TA], pbank[:, 0:TA], beff_t[:, m:m + 1],
                        xres[m][:, c0:c0 + TA], ALU.add, ALU.add), "v")
                    tok_pw[m % 2] = p.cnt["v"]
        p.barrier()
        if debug_phase == 3:
            emit_debug_and_return()
            return nc

        ph3b = ExitStack()
        with ph3b:
            s3b = lambda name, shape, dt=F32: ph3b.enter_context(nc.sbuf_tensor(name, shape, dt))
            gru_bt = [s3b(f"gru_bt{r}", [128, 2 * D], BF16) for r in range(8)]
            for r in range(8):
                p.dma("g", gru_bt[r][:], gruT[128 * r:128 * (r + 1), :])
            xg = s3b("xg", [128, 8 * TB], BF16)
            cbuf = s3b("cbuf", [128, 8 * TB])
            vbuf = s3b("vbuf", [128, 8 * TB])
            sgb = [s3b(f"sgb{i}", [128, TB]) for i in range(2)]
            hloc = s3b("hloc", [128, TB])
            tok_gru_s = [0, 0]; tok_gru_v = [0, 0]
            zbuf = s3b("zbuf", [128, TB])
            pbf = s3b("pbf", [128, 8 * TB], BF16)
            p.inc(nc.vector.memset(zbuf[:], 0.0), "v")

            tok_pd = 0
            for ci in range(NCB):
                c0 = TB * ci
                stats_guard()
                stats([xres[r][:, c0:c0 + TB] for r in range(8)], TB, rms_only=True)
                p.wait("v", "t")
                p.wait("v", "g")
                for r in range(8):
                    p.inc(nc.vector.scalar_tensor_tensor(
                        xg[:, r * TB:(r + 1) * TB], xres[r][:, c0:c0 + TB],
                        ggam[:, r:r + 1], ps_r[:, 0:TB], ALU.mult, ALU.mult), "v")
                stats_mark()
                vtok_xg = p.cnt["v"]
                p.wait("t", "v", vtok_xg)
                p.wait("t", "dg")
                for m in range(16):
                    pbank = ps_a if m % 2 == 0 else ps_b
                    p.wait("t", "s", tok_gru_s[m % 2])
                    p.wait("t", "v", tok_gru_v[m % 2])
                    for kk in range(8):
                        p.inc(nc.tensor.matmul(pbank[:, 0:TB],
                                               gru_bt[kk][:, 128 * m:128 * (m + 1)],
                                               xg[:, kk * TB:(kk + 1) * TB],
                                               start=(kk == 0), stop=(kk == 7)), "t")
                    r = m % 8
                    sl = slice(r * TB, (r + 1) * TB)
                    if m < 8:
                        p.wait("s", "t")
                        p.inc(nc.scalar.activation(sgb[m % 2][:], pbank[:, 0:TB],
                                                   AF.Sigmoid), "s")
                        p.wait("v", "s")
                        p.inc(nc.vector.scalar_tensor_tensor(vbuf[:, sl], pbank[:, 0:TB],
                                                             0.5, sgb[m % 2][:],
                                                             ALU.add, ALU.max), "v")
                        tok_gru_v[m % 2] = p.cnt["v"]
                        tok_gru_s[m % 2] = p.cnt["s"]
                    else:
                        p.wait("s", "t")
                        p.inc(nc.scalar.activation(cbuf[:, sl], pbank[:, 0:TB],
                                                   AF.Sigmoid, scale=-1.0), "s")
                        p.inc(nc.scalar.activation(sgb[m % 2][:], pbank[:, 0:TB],
                                                   AF.Sigmoid), "s")
                        tok_gru_s[m % 2] = p.cnt["s"]
                        p.wait("v", "s")
                        p.inc(nc.vector.tensor_mul(vbuf[:, sl], vbuf[:, sl],
                                                   sgb[m % 2][:]), "v")
                        tok_gru_v[m % 2] = p.cnt["v"]
                for r in range(8):
                    sl = slice(r * TB, (r + 1) * TB)
                    p.wait("v", "s")
                    p.inc(nc.vector.tensor_tensor_scan(
                        hloc[:], cbuf[:, sl], vbuf[:, sl],
                        gru_state[:, r:r + 1], ALU.mult, ALU.add), "v")
                    p.wait("s", "v")
                    p.inc(nc.scalar.copy(gru_state[:, r:r + 1], hloc[:, TB - 1:TB]), "s")
                    p.inc(nc.vector.tensor_add(xres[r][:, c0:c0 + TB],
                                               xres[r][:, c0:c0 + TB], hloc[:]), "v")
                    if tok_pd:
                        p.wait("v", "ds", tok_pd)
                    p.inc(nc.vector.tensor_tensor_scan(
                        pbf[:, sl], cbuf[:, sl], zbuf[:],
                        p_state[:, r:r + 1], ALU.mult, ALU.add), "v")
                    p.wait("s", "v")
                    p.inc(nc.scalar.copy(p_state[:, r:r + 1],
                                         pbf[:, (r + 1) * TB - 1:(r + 1) * TB]), "s")
                p.wait("q", "v")
                for r in range(8):
                    tok_pd = p.dma("q", p_save.ap()[128 * r:128 * (r + 1), c0:c0 + TB],
                                   pbf[:, r * TB:(r + 1) * TB])
            p.wait("q", "s")
            p.dma("q", gst_in.ap(), gru_state[:])
            p.wait("g", "ds")
            p.inc(nc.gpsimd.collective_compute(
                "AllGather", ALU.bypass, replica_groups=GROUPS,
                ins=[gst_in.ap()], outs=[gst_out.ap()]), "cc")
            p.wait("q", "cc")
            tG = p.dma("q", gst_sb[:], gst_out.ap()[0:128, :])
            p.wait("v", "ds", tG)
            p.inc(nc.vector.tensor_scalar(gst_sb[:], gst_sb[:], m_R[:, 0:1], None,
                                          ALU.mult), "v")
        p.barrier()
        ph3corr = ExitStack()
        with ph3corr:
            pcor = ph3corr.enter_context(nc.sbuf_tensor("pcor", [128, 8 * TA], BF16))
            for ci in range(NCA):
                c0 = TA * ci
                t = p.dma("q", r8(pcor[:]), d8(p_save.ap()[:, c0:c0 + TA]))
                p.wait("v", "ds", t)
                for r in range(8):
                    p.inc(nc.vector.scalar_tensor_tensor(
                        xres[r][:, c0:c0 + TA], pcor[:, r * TA:(r + 1) * TA],
                        gst_sb[:, r:r + 1], xres[r][:, c0:c0 + TA],
                        ALU.mult, ALU.add), "v")
                p.wait("q", "v")
                for r in range(8):
                    p.dma("q", z_save.ap()[128 * r:128 * (r + 1), c0:c0 + TA],
                          xres[r][:, c0:c0 + TA])
        p.barrier()
        if debug_phase == 35:
            emit_debug_and_return()
            return nc
        ph_res.close(); open_stacks.remove(ph_res)

        ph3c = ExitStack()
        with ph3c:
            s3c = lambda name, shape, dt=F32: ph3c.enter_context(nc.sbuf_tensor(name, shape, dt))
            w1q = [s3c(f"w1q{i}", [128, 8 * D], BF16) for i in range(3)]
            w2q = [s3c(f"w2q{i}", [128, 32 * 256], BF16) for i in range(3)]
            h1b = s3c("h1b", [128, 32 * TA], BF16)
            ln3 = [s3c(f"ln3_{i}", [128, 8 * TA], BF16) for i in range(2)]
            zc = [s3c(f"zc{i}", [128, 8 * TA]) for i in range(2)]

            def load_w1q(qi):
                return p.dma("g", r8(w1q[qi % 3][:]),
                             d8(ff1_bf.ap()[:, 1024 * (qi % 4):1024 * (qi % 4 + 1)]))

            def load_w2q(qi):
                return p.dma("s", r8(w2q[qi % 3][:], r=32),
                             d8(ff2_bf.ap()[:, 256 * (qi % 4):256 * (qi % 4 + 1)]))

            tz = [0, 0]
            tw1 = {}
            tw2 = {}
            tok_w1use = {-3: 0, -2: 0, -1: 0}
            tok_w2use = {-3: 0, -2: 0, -1: 0}
            tok_ff1 = [0, 0]
            tok_ff2 = [0, 0]
            tok_zuse = [0, 0]
            vtok_ln3 = [0, 0]

            def stats3c(ci):
                cur = ci % 2
                Z = zc[cur]
                Zs = [Z[:, r * TA:(r + 1) * TA] for r in range(8)]
                stats_guard()
                p.wait("t", "ds", tz[cur]); p.wait("s", "ds", tz[cur])
                stats(Zs, TA, rms_only=True)
                p.wait("v", "t")
                for r in range(8):
                    p.inc(nc.vector.scalar_tensor_tensor(
                        ln3[cur][:, r * TA:(r + 1) * TA], Zs[r],
                        fgam[:, r:r + 1], ps_r[:, 0:TA], ALU.mult, ALU.mult), "v")
                stats_mark()
                vtok_ln3[cur] = p.cnt["v"]

            tz[0] = p.dma("q", r8(zc[0][:]), d8(z_save.ap()[:, 0:TA]))
            tz[1] = p.dma("q", r8(zc[1][:]), d8(z_save.ap()[:, TA:2 * TA]))
            tw1[0] = load_w1q(0); tw1[1] = load_w1q(1)
            tw2[0] = load_w2q(0); tw2[1] = load_w2q(1)
            stats3c(0)
            for ci in range(NCA):
                cur = ci % 2
                Z = zc[cur]
                l3 = ln3[cur]
                for qi4 in range(4):
                    qi = 4 * ci + qi4
                    p.wait("t", "dg", tw1[qi])
                    p.wait("t", "v", vtok_ln3[cur])
                    for mm in range(8):
                        m = qi4 * 8 + mm
                        pbank = ps_a if mm % 2 == 0 else ps_b
                        p.wait("t", "s", tok_ff1[mm % 2])
                        for kk in range(8):
                            p.inc(nc.tensor.matmul(
                                pbank[:, 0:TA],
                                w1q[qi % 3][:, kk * 1024 + 128 * mm:kk * 1024 + 128 * (mm + 1)],
                                l3[:, kk * TA:(kk + 1) * TA],
                                start=(kk == 0), stop=(kk == 7)), "t")
                        p.wait("s", "t")
                        p.inc(nc.scalar.activation(h1b[:, m * TA:(m + 1) * TA],
                                                   pbank[:, 0:TA], AF.Gelu,
                                                   bias=b1_t[:, m:m + 1]), "s")
                        tok_ff1[mm % 2] = p.cnt["s"]
                    tok_w1use[qi] = p.cnt["t"]
                    if qi + 2 < 4 * NCA:
                        p.wait("g", "t", tok_w1use.get(qi - 1, 0))
                        tw1[qi + 2] = load_w1q(qi + 2)
                stok_h1 = p.cnt["s"]
                if ci + 1 < NCA:
                    stats3c(ci + 1)
                for qi4 in range(4):
                    qi = 4 * ci + qi4
                    p.wait("t", "da", tw2[qi])
                    p.wait("t", "s", stok_h1)
                    for mm in range(2):
                        m = qi4 * 2 + mm
                        pbank = ps_c if mm % 2 == 0 else ps_d
                        p.wait("t", "v", tok_ff2[mm % 2])
                        for kk in range(32):
                            p.inc(nc.tensor.matmul(
                                pbank[:, 0:TA],
                                w2q[qi % 3][:, kk * 256 + 128 * mm:kk * 256 + 128 * (mm + 1)],
                                h1b[:, kk * TA:(kk + 1) * TA],
                                start=(kk == 0), stop=(kk == 31)), "t")
                        p.wait("v", "t")
                        p.inc(nc.vector.scalar_tensor_tensor(
                            Z[:, m * TA:(m + 1) * TA], pbank[:, 0:TA], b2_t[:, m:m + 1],
                            Z[:, m * TA:(m + 1) * TA], ALU.add, ALU.add), "v")
                        tok_ff2[mm % 2] = p.cnt["v"]
                    tok_w2use[qi] = p.cnt["t"]
                    if qi + 2 < 4 * NCA:
                        p.wait("s", "t", tok_w2use.get(qi - 1, 0))
                        tw2[qi + 2] = load_w2q(qi + 2)
                tok_zuse[cur] = p.cnt["v"]
                p.wait("q", "v")
                for r in range(8):
                    p.dma("q", z_save.ap()[128 * r:128 * (r + 1), TA * ci:TA * (ci + 1)],
                          Z[:, r * TA:(r + 1) * TA])
                if ci + 2 < NCA:
                    tz[cur] = p.dma("q", r8(zc[cur][:]),
                                    d8(z_save.ap()[:, TA * (ci + 2):TA * (ci + 3)]))
        p.barrier()

        ph4 = ExitStack()
        with ph4:
            s4 = lambda name, shape, dt=F32: ph4.enter_context(nc.sbuf_tensor(name, shape, dt))
            sgn_b = [s4(f"sgn4_{r}", [128, D], BF16) for r in range(8)]
            lnb = s4("lnb4", [128, 8 * TA], BF16)
            load_sign(tmpw, sgn_b)
            p.barrier()
            x3c = [s4(f"x3c{i}", [128, 8 * TA]) for i in range(2)]
            zc4 = s4("zc4", [128, 8 * TA])
            tok_p4 = [0, 0]
            for ci in range(NCA):
                cur = ci % 2
                c0 = TA * ci
                p.wait("q", "v"); p.wait("q", "s"); p.wait("q", "t")
                tS = p.dma("q", r8(x3c[cur][:]), d8(x2_save.ap()[:, c0:c0 + TA]))
                tZ = p.dma("q", r8(zc4[:]), d8(z_save.ap()[:, c0:c0 + TA]))
                p.wait("v", "ds", tZ)
                for r in range(8):
                    sl = slice(r * TA, (r + 1) * TA)
                    p.inc(nc.vector.tensor_add(x3c[cur][:, sl], x3c[cur][:, sl],
                                               zc4[:, sl]), "v")
                vtok_x3 = p.cnt["v"]
                X3 = [x3c[cur][:, r * TA:(r + 1) * TA] for r in range(8)]
                if debug_phase == 4:
                    p.wait("q", "v")
                    for r in range(8):
                        p.dma("q", yout[128 * r:128 * (r + 1), c0:c0 + TA],
                              zc4[:, r * TA:(r + 1) * TA])
                    continue
                stats_guard()
                p.wait("t", "v", vtok_x3)
                stats(X3, TA, rms_only=False)
                ln_apply_bf16(X3, TA, lnb, dst_T=TA)
                p.wait("t", "v", p.cnt["v"])
                for m in range(8):
                    pbank = ps_a if m % 2 == 0 else ps_b
                    p.wait("t", "v", tok_p4[m % 2])
                    for kk in range(8):
                        p.inc(nc.tensor.matmul(pbank[:, 0:TA],
                                               sgn_b[kk][:, 128 * m:128 * (m + 1)],
                                               lnb[:, kk * TA:kk * TA + TA],
                                               start=(kk == 0), stop=(kk == 7)), "t")
                    p.wait("v", "t")
                    p.inc(nc.vector.tensor_add(x3c[cur][:, m * TA:(m + 1) * TA],
                                               x3c[cur][:, m * TA:(m + 1) * TA],
                                               pbank[:, 0:TA]), "v")
                    tok_p4[m % 2] = p.cnt["v"]
                    p.wait("q", "v")
                    p.dma("q", yout[128 * m:128 * (m + 1), c0:c0 + TA],
                          x3c[cur][:, m * TA:(m + 1) * TA])

        p.wait("q", "ds"); p.wait("q", "da"); p.wait("q", "dg")
    return nc


_NC_CACHE = {}


def _get_nc():
    key = DEBUG_PHASE
    if key not in _NC_CACHE:
        _NC_CACHE[key] = build_program(DEBUG_PHASE)
    return _NC_CACHE[key]


def _prep_core_inputs(c, full):
    b, q = c // 2, c % 2
    x = full["L0_attn_a_x"]
    xb = np.ascontiguousarray(x[b].T)
    perm = np.concatenate([np.arange(512 * q, 512 * q + 512),
                           np.arange(512 * (1 - q), 512 * (1 - q) + 512)])
    xT = np.ascontiguousarray(xb[perm])
    x_mine = np.ascontiguousarray(xb[:, HL * q:HL * (q + 1)])
    A_log = np.asarray(full["L0_attn_a_A_log"], np.float32)
    heads = np.arange(8 * q, 8 * q + 8)
    a_log_arr = np.zeros((128, NPAIR * NS), np.float32)
    for k in range(NPAIR):
        hA, hB = heads[2 * k], heads[2 * k + 1]
        a_log_arr[0:64, k * NS:(k + 1) * NS] = A_log[hA][None, :]
        a_log_arr[64:128, k * NS:(k + 1) * NS] = A_log[hB][None, :]
    bdt = np.asarray(full["L0_attn_a_bdt"], np.float32)
    bdt_pair = np.concatenate([bdt, bdt]).reshape(128, 1)

    col8 = lambda v: np.ascontiguousarray(np.asarray(v, np.float32).reshape(8, 128).T)
    col32 = lambda v: np.ascontiguousarray(np.asarray(v, np.float32).reshape(32, 128).T)
    dww = np.asarray(full["L0_attn_b_conv_dw_w"], np.float32).reshape(1024, 3)
    convw = np.zeros((128, 24), np.float32)
    for r in range(8):
        convw[:, 3 * r:3 * r + 3] = dww[128 * r:128 * (r + 1), :]

    ct = lambda a: np.ascontiguousarray(np.asarray(a, np.float32).T)
    return dict(
        xT=xT, x_mine=x_mine,
        wxT=ct(full["L0_attn_a_Wx"]),
        wx_nat=np.ascontiguousarray(np.asarray(full["L0_attn_a_Wx"], np.float32)),
        wdtT=ct(full["L0_attn_a_Wdt"]),
        a_log_arr=a_log_arr, bdt_pair=bdt_pair,
        woutT=ct(full["L0_attn_a_Wout"]),
        bout_col=col8(full["L0_attn_a_bout"]),
        sgnT=ct(full["L0_ffn_weight"]),
        convw=convw,
        convb_col=col8(full["L0_attn_b_conv_dw_b"]),
        pwb_col=col8(full["L0_attn_b_conv_pw_b"]),
        conv_g=col8(full["L0_attn_b_conv_norm_g"]),
        gru_g=col8(full["L0_attn_b_gru_norm_g"]),
        ff_g=col8(full["L0_attn_b_ff_norm_g"]),
        pwT=ct(full["L0_attn_b_conv_pw_w"]),
        gruT=ct(full["L0_attn_b_gru_w"]),
        ff1T=ct(full["L0_attn_b_ff_w1"]),
        ff2T=ct(full["L0_attn_b_ff_w2"]),
        b1_col=col32(full["L0_attn_b_ff_b1"]),
        b2_col=col8(full["L0_attn_b_ff_b2"]),
        maskA=np.full((128, 1), 1.0 if q == 0 else 0.0, np.float32),
        maskB=np.full((128, 1), 0.0 if q == 0 else 1.0, np.float32),
        mrecv=np.full((128, 1), 1.0 if q == 1 else 0.0, np.float32),
    )


def kernel(**inputs):
    nc = _get_nc()
    in_maps = [_prep_core_inputs(c, inputs) for c in range(NCORES)]
    res = run_bass_kernel_spmd(nc, in_maps, core_ids=list(range(NCORES)))
    out = np.empty((4, L, D), np.float32)
    for c in range(NCORES):
        b, q = c // 2, c % 2
        out[b, HL * q:HL * (q + 1), :] = res.results[c]["yout"].T
    return out


# version 16
# speedup vs baseline: 1.0411x; 1.0411x over previous
import numpy as np
from contextlib import ExitStack

import concourse.bass as bass
import concourse.mybir as mybir
from concourse.bass_utils import run_bass_kernel_spmd

F32 = mybir.dt.float32
BF16 = mybir.dt.bfloat16
AF = mybir.ActivationFunctionType
ALU = mybir.AluOpType

NCORES = 8
D = 1024
L = 8192
HL = 4096
NS = 16
RK = 64
DH = 64
T1 = 512
NCH1 = L // T1
TA = 512
NCA = HL // TA
TB = 256
NCB = HL // TB
NPAIR = 4
GROUPS = [[0, 1], [2, 3], [4, 5], [6, 7]]

DEBUG_PHASE = 0

ENGS = ("t", "s", "v", "g", "q")


class P:

    def __init__(self, nc, stack):
        self.nc = nc
        self.e = {"t": nc.tensor, "s": nc.scalar, "v": nc.vector,
                  "g": nc.gpsimd, "q": nc.sync}
        names = ["t", "s", "v", "g", "ds", "dg", "da", "cc"]
        self.sem = {n: stack.enter_context(nc.semaphore("sem_" + n)) for n in names}
        self.cnt = {n: 0 for n in names}
        self.last_wait = {}

    def inc(self, inst, sem, n=1):
        inst.then_inc(self.sem[sem], n)
        self.cnt[sem] += n
        return self.cnt[sem]

    def wait(self, eng, sem, val=None):
        if val is None:
            val = self.cnt[sem]
        if val <= 0:
            return
        key = (eng, sem)
        if self.last_wait.get(key, -1) >= val:
            return
        self.e[eng].wait_ge(self.sem[sem], val)
        self.last_wait[key] = val

    def dma(self, queue, out, in_):
        smap = {"q": "ds", "g": "dg", "s": "da"}
        s = smap[queue]
        inst = self.e[queue].dma_start(out, in_)
        return self.inc(inst, s, 16)

    def barrier(self):
        snap = dict(self.cnt)
        for eng in ENGS:
            for s, v in snap.items():
                self.wait(eng, s, v)


def r8(sb_ap, r=8):
    return sb_ap.rearrange("p (r t) -> p r t", r=r)


def d8(dram_ap, p=128):
    return dram_ap.rearrange("(r p) t -> p r t", p=p)


def build_program(debug_phase=0):
    nc = bass.Bass("TRN2", target_bir_lowering=False)
    ap = lambda h: h.ap()

    xT = ap(nc.dram_tensor("xT", [D, L], F32, kind="ExternalInput"))
    x_mine = ap(nc.dram_tensor("x_mine", [D, HL], F32, kind="ExternalInput"))
    wxT = ap(nc.dram_tensor("wxT", [DH, 96], F32, kind="ExternalInput"))
    wx_nat = ap(nc.dram_tensor("wx_nat", [96, DH], F32, kind="ExternalInput"))
    wdtT = ap(nc.dram_tensor("wdtT", [RK, DH], F32, kind="ExternalInput"))
    a_log_arr = ap(nc.dram_tensor("a_log_arr", [128, NPAIR * NS], F32, kind="ExternalInput"))
    bdt_pair = ap(nc.dram_tensor("bdt_pair", [128, 1], F32, kind="ExternalInput"))
    woutT = ap(nc.dram_tensor("woutT", [D, D], F32, kind="ExternalInput"))
    bout_col = ap(nc.dram_tensor("bout_col", [128, 8], F32, kind="ExternalInput"))
    sgnT = ap(nc.dram_tensor("sgnT", [D, D], F32, kind="ExternalInput"))
    convw = ap(nc.dram_tensor("convw", [128, 24], F32, kind="ExternalInput"))
    convb_col = ap(nc.dram_tensor("convb_col", [128, 8], F32, kind="ExternalInput"))
    pwb_col = ap(nc.dram_tensor("pwb_col", [128, 8], F32, kind="ExternalInput"))
    conv_g = ap(nc.dram_tensor("conv_g", [128, 8], F32, kind="ExternalInput"))
    gru_g = ap(nc.dram_tensor("gru_g", [128, 8], F32, kind="ExternalInput"))
    ff_g = ap(nc.dram_tensor("ff_g", [128, 8], F32, kind="ExternalInput"))
    pwT = ap(nc.dram_tensor("pwT", [D, D], F32, kind="ExternalInput"))
    gruT = ap(nc.dram_tensor("gruT", [D, 2 * D], F32, kind="ExternalInput"))
    ff1T = ap(nc.dram_tensor("ff1T", [D, 4 * D], F32, kind="ExternalInput"))
    ff2T = ap(nc.dram_tensor("ff2T", [4 * D, D], F32, kind="ExternalInput"))
    b1_col = ap(nc.dram_tensor("b1_col", [128, 32], F32, kind="ExternalInput"))
    b2_col = ap(nc.dram_tensor("b2_col", [128, 8], F32, kind="ExternalInput"))
    maskA = ap(nc.dram_tensor("maskA", [128, 1], F32, kind="ExternalInput"))
    maskB = ap(nc.dram_tensor("maskB", [128, 1], F32, kind="ExternalInput"))
    mrecv = ap(nc.dram_tensor("mrecv", [128, 1], F32, kind="ExternalInput"))

    yout = ap(nc.dram_tensor("yout", [D, HL], F32, kind="ExternalOutput"))
    taps = {}
    if debug_phase == 11:
        taps["ln1f"] = ap(nc.dram_tensor("tap_ln1f", [128, 4 * T1], F32, kind="ExternalOutput"))
        taps["dt"] = ap(nc.dram_tensor("tap_dt", [128, T1], F32, kind="ExternalOutput"))
        taps["dtx"] = ap(nc.dram_tensor("tap_dtx", [128, T1], F32, kind="ExternalOutput"))
        taps["hbuf"] = ap(nc.dram_tensor("tap_hbuf", [128, NS * T1], F32, kind="ExternalOutput"))
        taps["abig"] = ap(nc.dram_tensor("tap_abig", [128, NS * T1], F32, kind="ExternalOutput"))
        taps["stb"] = ap(nc.dram_tensor("tap_stb", [128, 256], F32, kind="ExternalOutput"))
        taps["bexp0"] = ap(nc.dram_tensor("tap_bexp0", [128, T1], F32, kind="ExternalOutput"))
        taps["ubig"] = ap(nc.dram_tensor("tap_ubig", [128, NS * T1], F32, kind="ExternalOutput"))
        taps["y0"] = ap(nc.dram_tensor("tap_y0", [128, T1], F32, kind="ExternalOutput"))
        taps["aarr"] = ap(nc.dram_tensor("tap_aarr", [128, NPAIR * NS], F32, kind="ExternalOutput"))
        taps["stdt"] = ap(nc.dram_tensor("tap_stdt", [128, 128], F32, kind="ExternalOutput"))

    y_stage = nc.dram_tensor("y_stage", [512, L], BF16)
    y_gath = nc.dram_tensor("y_gath", [1024, L], BF16)
    halo_in = nc.dram_tensor("halo_in", [128, 16], F32)
    halo_out = nc.dram_tensor("halo_out", [256, 16], F32)
    gst_in = nc.dram_tensor("gst_in", [128, 8], F32)
    gst_out = nc.dram_tensor("gst_out", [256, 8], F32)
    x2_save = nc.dram_tensor("x2_save", [D, HL], F32)
    z_save = nc.dram_tensor("z_save", [D, HL], F32)
    p_save = nc.dram_tensor("p_save", [D, HL], BF16)
    ff1_bf = nc.dram_tensor("ff1_bf", [D, 4 * D], BF16)
    ff2_bf = nc.dram_tensor("ff2_bf", [4 * D, D], BF16)

    outer = ExitStack()
    with outer:
        p = P(nc, outer)
        sb = lambda name, shape, dt=F32: outer.enter_context(nc.sbuf_tensor(name, shape, dt))
        psum = lambda name: outer.enter_context(nc.psum_tensor(name, [128, 512], F32))

        ones_col = sb("ones_col", [128, 1])
        ones_row = sb("ones_row", [1, 128])
        m_A = sb("m_A", [128, 1]); m_B = sb("m_B", [128, 1]); m_R = sb("m_R", [128, 1])
        a_arr = sb("a_arr", [128, NPAIR * NS])
        bdt_t = sb("bdt_t", [128, 1])
        bout_t = sb("bout_t", [128, 8])
        cgam = sb("cgam", [128, 8]); ggam = sb("ggam", [128, 8]); fgam = sb("fgam", [128, 8])
        convw_t = sb("convw_t", [128, 24])
        convb_t = sb("convb_t", [128, 8]); pwb_t = sb("pwb_t", [128, 8])
        beff_t = sb("beff_t", [128, 8])
        b1_t = sb("b1_t", [128, 32]); b2_t = sb("b2_t", [128, 8])
        ssm_state = sb("ssm_state", [128, NPAIR * NS])
        gru_state = sb("gru_state", [128, 8])
        p_state = sb("p_state", [128, 8])
        gst_sb = sb("gst_sb", [128, 8])
        row = sb("row", [1, 4 * 512])
        cst = sb("cst", [128, 4])
        tmpw = sb("tmpw", [128, D])

        ps_s1 = psum("ps_s1"); ps_s2 = psum("ps_s2")
        ps_r = psum("ps_r"); ps_m = psum("ps_m")
        ps_a = psum("ps_a"); ps_b = psum("ps_b")
        ps_c = psum("ps_c"); ps_d = psum("ps_d")

        p.inc(nc.vector.memset(cst[:, 0:1], 1e-5), "v")
        p.inc(nc.vector.memset(cst[:, 1:2], -0.5), "v")
        p.inc(nc.vector.memset(cst[:, 2:3], 1.0 / D), "v")
        p.inc(nc.vector.memset(cst[:, 3:4], 32.0), "v")
        p.dma("q", m_A[:], maskA); p.dma("q", m_B[:], maskB); p.dma("q", m_R[:], mrecv)
        p.dma("q", bdt_t[:], bdt_pair); p.dma("q", bout_t[:], bout_col)
        p.dma("q", convw_t[:], convw); p.dma("q", convb_t[:], convb_col)
        p.dma("q", pwb_t[:], pwb_col)
        p.dma("q", b1_t[:], b1_col); p.dma("q", b2_t[:], b2_col)
        tmp_al = sb("tmp_al", [128, NPAIR * NS])
        tmp_g = sb("tmp_g", [128, 24])
        p.dma("q", tmp_al[:], a_log_arr)
        p.dma("q", tmp_g[:, 0:8], conv_g); p.dma("q", tmp_g[:, 8:16], gru_g)
        p.dma("q", tmp_g[:, 16:24], ff_g)
        wx_t = sb("wx_t", [96, DH]); wdt_t = sb("wdt_t", [RK, DH])
        p.dma("q", wx_t[:], wx_nat); p.dma("q", wdt_t[:], wdtT)

        p.wait("s", "ds"); p.wait("s", "dg"); p.wait("s", "v")
        p.inc(nc.scalar.activation(a_arr[:], tmp_al[:], AF.Exp), "s")
        p.inc(nc.scalar.mul(a_arr[:], a_arr[:], -1.0), "s")
        p.wait("v", "ds")
        p.inc(nc.vector.tensor_scalar(cgam[:], tmp_g[:, 0:8], 32.0, 32.0,
                                      ALU.mult, ALU.add), "v")
        p.inc(nc.vector.tensor_scalar(ggam[:], tmp_g[:, 8:16], 32.0, 32.0,
                                      ALU.mult, ALU.add), "v")
        p.inc(nc.vector.tensor_scalar(fgam[:], tmp_g[:, 16:24], 32.0, 32.0,
                                      ALU.mult, ALU.add), "v")
        p.inc(nc.vector.memset(ssm_state[:], 0.0), "v")
        p.inc(nc.vector.memset(gru_state[:], 0.0), "v")
        p.inc(nc.vector.memset(p_state[:], 1.0), "v")
        p.inc(nc.vector.memset(ones_col[:], 1.0), "v")
        p.inc(nc.vector.memset(ones_row[:], 1.0), "v")

        p.barrier()

        def stats(src_slices, T, rms_only):
            sqb = stats.sqb
            p.wait("s", "v")
            if not rms_only:
                for r in range(8):
                    p.inc(nc.tensor.matmul(ps_s1[0:1, 0:T], ones_col[:], src_slices[r],
                                           start=(r == 0), stop=(r == 7)), "t")
            for r in range(8):
                sq = sqb[r % 2]
                if stats.tok_sq[r % 2]:
                    p.wait("s", "t", stats.tok_sq[r % 2])
                p.inc(nc.scalar.activation(sq[:, 0:T], src_slices[r], AF.Square), "s")
                p.wait("t", "s")
                stats.tok_sq[r % 2] = p.inc(
                    nc.tensor.matmul(ps_s2[0:1, 0:T], ones_col[:], sq[:, 0:T],
                                     start=(r == 0), stop=(r == 7)), "t")
            if rms_only:
                p.wait("v", "t")
                p.inc(nc.vector.tensor_scalar_max(row[0:1, 0:T], ps_s2[0:1, 0:T], 1e-24), "v")
                p.wait("s", "v")
                p.inc(nc.scalar.activation(row[0:1, 512:512 + T], row[0:1, 0:T], AF.Ln), "s")
                p.inc(nc.scalar.activation(row[0:1, 512:512 + T], row[0:1, 512:512 + T],
                                           AF.Exp, scale=cst[0:1, 1:2]), "s")
                p.wait("t", "s")
                p.inc(nc.tensor.matmul(ps_r[:, 0:T], ones_row[0:1, :], row[0:1, 512:512 + T],
                                       start=True, stop=True), "t")
                return
            p.wait("s", "t")
            p.inc(nc.scalar.activation(row[0:1, 0:T], ps_s1[0:1, 0:T], AF.Square,
                                       scale=cst[0:1, 2:3]), "s")
            p.wait("v", "s")
            p.inc(nc.vector.scalar_tensor_tensor(row[0:1, 512:512 + T], ps_s2[0:1, 0:T],
                                                 1.0 / D, row[0:1, 0:T],
                                                 ALU.mult, ALU.subtract), "v")
            p.wait("s", "v")
            p.inc(nc.scalar.activation(row[0:1, 1024:1024 + T], row[0:1, 512:512 + T],
                                       AF.Ln, bias=cst[0:1, 0:1]), "s")
            p.inc(nc.scalar.activation(row[0:1, 1024:1024 + T], row[0:1, 1024:1024 + T],
                                       AF.Exp, scale=cst[0:1, 1:2]), "s")
            p.wait("v", "s")
            p.inc(nc.vector.scalar_tensor_tensor(row[0:1, 1536:1536 + T], ps_s1[0:1, 0:T],
                                                 -1.0 / D, row[0:1, 1024:1024 + T],
                                                 ALU.mult, ALU.mult), "v")
            p.wait("t", "v")
            p.inc(nc.tensor.matmul(ps_r[:, 0:T], ones_row[0:1, :], row[0:1, 1024:1024 + T],
                                   start=True, stop=True), "t")
            p.inc(nc.tensor.matmul(ps_m[:, 0:T], ones_row[0:1, :], row[0:1, 1536:1536 + T],
                                   start=True, stop=True), "t")

        stats.sqb = [sb("sqs0", [128, 512]), sb("sqs1", [128, 512])]
        stats.tok_sq = [0, 0]
        stats.cons = {"s": 0, "v": 0}

        def stats_mark():
            stats.cons = {"s": p.cnt["s"], "v": p.cnt["v"]}

        def stats_guard():
            p.wait("t", "s", stats.cons["s"])
            p.wait("t", "v", stats.cons["v"])

        ph_st = ExitStack()
        sbst = lambda name, shape, dt=F32: ph_st.enter_context(nc.sbuf_tensor(name, shape, dt))
        st_dt = sbst("st_dt", [128, 128], BF16)
        st_B = sbst("st_B", [128, 128], BF16)
        st_C = sbst("st_C", [128, 128], BF16)
        wx_t2 = sbst("wx_t2", [128, 96])
        p.dma("q", wx_t2[0:64, :], wxT)
        p.dma("q", wx_t2[64:128, :], wxT)
        p.wait("t", "ds")
        p.inc(nc.tensor.matmul(ps_a[0:DH, 0:DH], wx_t[0:RK, 0:DH], wdt_t[:],
                               start=True, stop=True), "t")
        p.inc(nc.vector.memset(st_dt[:], 0.0), "v")
        p.inc(nc.vector.memset(st_B[:], 0.0), "v")
        p.inc(nc.vector.memset(st_C[:], 0.0), "v")
        p.wait("v", "t")
        p.inc(nc.vector.tensor_copy(st_dt[0:DH, 0:DH], ps_a[0:DH, 0:DH]), "v")
        p.inc(nc.vector.tensor_copy(st_dt[DH:128, DH:128], ps_a[0:DH, 0:DH]), "v")
        p.wait("v", "ds")
        for (st, c0) in ((st_B, RK), (st_C, RK + NS)):
            p.inc(nc.vector.tensor_copy(st[0:DH, 0:NS], wx_t2[0:DH, c0:c0 + NS]), "v")
            p.inc(nc.vector.tensor_copy(st[0:DH, 2 * NS:3 * NS], wx_t2[0:DH, c0:c0 + NS]), "v")
            p.inc(nc.vector.tensor_copy(st[DH:128, 4 * NS:5 * NS],
                                        wx_t2[DH:128, c0:c0 + NS]), "v")
            p.inc(nc.vector.tensor_copy(st[DH:128, 6 * NS:7 * NS],
                                        wx_t2[DH:128, c0:c0 + NS]), "v")

        if debug_phase == 11:
            stdt_f = sbst("stdt_f", [128, 128])
            p.wait("s", "v")
            p.inc(nc.scalar.copy(stdt_f[:], st_dt[:]), "s")
            p.wait("q", "s")
            p.dma("q", taps["stdt"], stdt_f[:])
        ph1 = ExitStack()
        with ph1:
            s1 = lambda name, shape, dt=F32: ph1.enter_context(nc.sbuf_tensor(name, shape, dt))
            xc = [s1(f"xc{i}", [128, 8 * T1]) for i in range(2)]
            ln1f = [s1(f"ln1f{i}", [128, 4 * T1]) for i in range(2)]
            ln1b = [s1(f"ln1b{i}", [128, 4 * T1], BF16) for i in range(2)]
            dt_f = s1("dt_f", [128, T1]); dtx_f = s1("dtx_f", [128, T1], BF16)
            Barr = s1("Barr", [128, T1], BF16); Carr = s1("Carr", [128, T1], BF16)
            bexp = s1("bexp", [128, T1], BF16); cexp = s1("cexp", [128, T1], BF16)
            sp1 = s1("sp1", [128, T1]); sp2 = s1("sp2", [128, T1])
            abig = s1("abig", [128, NS * T1])
            ubuf_all = s1("ubuf_all", [128, NS * T1], BF16)
            hbuf = s1("hbuf", [128, NS * T1])
            hcbuf = [s1(f"hcbuf{i}", [128, NS * T1], BF16) for i in range(2)]
            ybf = [s1(f"ybf{k}", [128, T1], BF16) for k in range(NPAIR)]

            exp_tok = [0] * NS
            abig_war_tok = [0]
            barr_cons_tok = [0]
            tree_tok = [0, 0]
            last_state_tok = 0
            tok_xc = [(0, 0, 0)] * 2
            tok_ydma = [0] * NPAIR
            prev_sp_tok = [0]
            ld_tok = [0, 0]

            ld_tok[0] = p.dma("q", r8(xc[0][:]), d8(xT[:, 0:T1]))
            stok_lnb_c = [0, 0]

            def ln1_chunk(ci):
                cur = ci % 2
                X = xc[cur]
                Xs = [X[:, r * T1:(r + 1) * T1] for r in range(8)]
                stats_guard()
                p.wait("t", "ds", ld_tok[cur])
                p.wait("s", "ds", ld_tok[cur])
                stats(Xs, T1, rms_only=False)
                p.wait("v", "t")
                for k in range(NPAIR):
                    sl = slice(k * T1, (k + 1) * T1)
                    p.inc(nc.vector.tensor_mul(ln1f[cur][:, sl], Xs[k],
                                               ps_r[:, 0:T1]), "v")
                    p.inc(nc.vector.tensor_add(ln1f[cur][:, sl], ln1f[cur][:, sl],
                                               ps_m[:, 0:T1]), "v")
                vtok_ln = p.cnt["v"]
                tok_xc[cur] = (p.cnt["t"], p.cnt["s"], vtok_ln)
                p.wait("s", "v", vtok_ln)
                p.wait("s", "t")
                for k in range(NPAIR):
                    sl = slice(k * T1, (k + 1) * T1)
                    p.inc(nc.scalar.copy(ln1b[cur][:, sl], ln1f[cur][:, sl]), "s")
                stok_lnb_c[cur] = p.cnt["s"]
                stats_mark()

            ln1_chunk(0)
            for ci in range(NCH1):
                cur = ci % 2
                nxt = (ci + 1) % 2
                if ci + 1 < NCH1:
                    tt, ts, tv = tok_xc[nxt]
                    p.wait("q", "t", tt); p.wait("q", "s", ts); p.wait("q", "v", tv)
                    ld_tok[nxt] = p.dma("q", r8(xc[nxt][:]),
                                        d8(xT[:, T1 * (ci + 1):T1 * (ci + 2)]))
                stok_lnb = stok_lnb_c[cur]

                for k in range(NPAIR):
                    if k == 1 and ci + 1 < NCH1:
                        ln1_chunk(ci + 1)
                    sl = slice(k * T1, (k + 1) * T1)
                    p.wait("t", "s", stok_lnb)
                    p.wait("t", "s", prev_sp_tok[0])
                    p.inc(nc.tensor.matmul(ps_a[:, 0:T1], st_dt[:], ln1b[cur][:, sl],
                                           start=True, stop=True), "t")
                    p.inc(nc.tensor.matmul(ps_b[:, 0:T1], st_B[:], ln1b[cur][:, sl],
                                           start=True, stop=True), "t")
                    p.inc(nc.tensor.matmul(ps_c[:, 0:T1], st_C[:], ln1b[cur][:, sl],
                                           start=True, stop=True), "t")
                    t_dummy = p.inc(nc.tensor.matmul(ps_d[:, 0:T1], st_dt[:],
                                                     ln1b[cur][:, sl],
                                                     start=True, stop=True), "t")
                    p.wait("s", "t", t_dummy)
                    p.wait("s", "v", barr_cons_tok[0])
                    p.inc(nc.scalar.activation(sp1[:], ps_a[:, 0:T1], AF.Abs,
                                               bias=bdt_t[:, 0:1]), "s")
                    p.inc(nc.scalar.activation(sp1[:], sp1[:], AF.Exp, scale=-1.0), "s")
                    p.inc(nc.scalar.activation(sp1[:], sp1[:], AF.Ln, bias=1.0), "s")
                    p.inc(nc.scalar.activation(sp2[:], ps_a[:, 0:T1], AF.Relu,
                                               bias=bdt_t[:, 0:1]), "s")
                    p.inc(nc.scalar.copy(Barr[:], ps_b[:, 0:T1]), "s")
                    p.inc(nc.scalar.copy(Carr[:], ps_c[:, 0:T1]), "s")
                    stok_sp = p.cnt["s"]
                    stok_bc = stok_sp
                    prev_sp_tok[0] = stok_sp
                    p.wait("v", "s", stok_sp)
                    p.inc(nc.vector.tensor_add(dt_f[:], sp1[:], sp2[:]), "v")
                    p.inc(nc.vector.tensor_mul(dtx_f[:], dt_f[:], ln1f[cur][:, sl]), "v")
                    vtok_dtx = p.cnt["v"]
                    p.wait("s", "v", abig_war_tok[0])
                    p.wait("s", "v", vtok_dtx)
                    for n in range(NS):
                        p.inc(nc.scalar.activation(
                            abig[:, n * T1:(n + 1) * T1], dt_f[:], AF.Exp,
                            scale=a_arr[:, k * NS + n:k * NS + n + 1]), "s")
                        exp_tok[n] = p.cnt["s"]
                    hcb = hcbuf[k % 2]
                    p.wait("v", "g", tree_tok[k % 2])
                    U32 = mybir.dt.uint32
                    bc = lambda apx: apx.bitcast(U32)
                    p.wait("v", "s", stok_bc)
                    for n in range(NS):
                        p.inc(nc.vector.stream_shuffle(bc(bexp[:]), bc(Barr[:]),
                                                       [n] * 32), "v")
                        p.inc(nc.vector.tensor_mul(ubuf_all[:, n * T1:(n + 1) * T1],
                                                   bexp[:], dtx_f[:]), "v")
                    p.wait("v", "s", exp_tok[NS - 1])
                    for n in range(NS):
                        c = n * T1
                        p.inc(nc.vector.tensor_tensor_scan(
                            hbuf[:, c:c + T1], abig[:, c:c + T1],
                            ubuf_all[:, c:c + T1],
                            ssm_state[:, k * NS + n:k * NS + n + 1],
                            ALU.mult, ALU.add), "v")
                    scan_tok = p.cnt["v"]
                    abig_war_tok[0] = scan_tok
                    for n in range(NS):
                        p.inc(nc.vector.stream_shuffle(bc(cexp[:]), bc(Carr[:]),
                                                       [n] * 32), "v")
                        p.inc(nc.vector.tensor_mul(hcb[:, n * T1:(n + 1) * T1],
                                                   hbuf[:, n * T1:(n + 1) * T1],
                                                   cexp[:]), "v")
                    barr_cons_tok[0] = p.cnt["v"]
                    if debug_phase == 11 and ci == 0 and k == 0:
                        p.wait("q", "v"); p.wait("q", "s")
                        p.wait("g", "v"); p.wait("g", "s")
                        p.dma("q", taps["ln1f"], ln1f[cur][:])
                        p.dma("q", taps["dt"], dt_f[:])
                        p.dma("g", taps["dtx"], dtx_f[:])
                        p.dma("g", taps["hbuf"], hbuf[:])
                        p.dma("q", taps["aarr"], a_arr[:])
                        p.wait("v", "ds"); p.wait("s", "ds"); p.wait("g", "ds")
                    p.wait("s", "v", scan_tok)
                    hview = hbuf[:].rearrange("p (n t) -> p n t", n=NS)
                    p.inc(nc.scalar.copy(ssm_state[:, k * NS:(k + 1) * NS],
                                         hview[:, :, T1 - 1:T1]), "s")
                    stok_state = p.cnt["s"]
                    last_state_tok = stok_state
                    p.wait("g", "v")
                    w = NS * T1
                    while w > T1:
                        h = w // 2
                        p.inc(nc.gpsimd.tensor_add(hcb[:, 0:h], hcb[:, 0:h],
                                                   hcb[:, h:w]), "g")
                        w = h
                    if tok_ydma[k]:
                        p.wait("g", "ds", tok_ydma[k])
                    p.inc(nc.gpsimd.tensor_copy(ybf[k][:], hcb[:, 0:T1]), "g")
                    tree_tok[k % 2] = p.cnt["g"]
                    if debug_phase == 11 and ci == 0 and k == 0:
                        p.wait("q", "g", tree_tok[k % 2])
                        p.dma("g", taps["y0"], hcb[:, 0:T1])
                        p.wait("v", "dg"); p.wait("s", "dg"); p.wait("g", "dg")
                    p.wait("q", "g", tree_tok[k % 2])
                    tok_ydma[k] = p.dma(
                        "q", y_stage.ap()[128 * k:128 * (k + 1), T1 * ci:T1 * (ci + 1)],
                        ybf[k][:])

            p.wait("g", "ds")
            p.inc(nc.gpsimd.collective_compute(
                "AllGather", ALU.bypass, replica_groups=GROUPS,
                ins=[y_stage.ap()], outs=[y_gath.ap()]), "cc")

        open_stacks = []
        ph_res = ExitStack(); open_stacks.append(ph_res)
        ph_st.close()
        xres = [ph_res.enter_context(nc.sbuf_tensor(f"xres{r}", [128, HL], F32))
                for r in range(8)]
        ph_w = ExitStack(); open_stacks.append(ph_w)
        wout_b = [ph_w.enter_context(nc.sbuf_tensor(f"wout_b{r}", [128, D], BF16))
                  for r in range(8)]
        for r in range(8):
            p.dma("g", wout_b[r][:], woutT[128 * r:128 * (r + 1), :])
        p.barrier()

        ph1b = ExitStack()
        with ph1b:
            s1b = lambda name, shape, dt=F32: ph1b.enter_context(nc.sbuf_tensor(name, shape, dt))
            yA = s1b("yA", [128, 8 * T1], BF16)
            yB = s1b("yB", [128, 8 * T1], BF16)
            ym = s1b("ym", [128, 8 * T1], BF16)
            xm = s1b("xm", [128, 8 * T1])
            tok_1b = [0, 0, 0, 0]
            p.wait("q", "cc")
            for ci in range(8):
                p.wait("q", "v")
                tA = p.dma("q", r8(yA[:]), d8(y_gath.ap()[:, T1 * ci:T1 * (ci + 1)]))
                tB = p.dma("q", r8(yB[:]), d8(y_gath.ap()[:, HL + T1 * ci:HL + T1 * (ci + 1)]))
                tX = p.dma("q", r8(xm[:]), d8(x_mine[:, T1 * ci:T1 * (ci + 1)]))
                p.wait("v", "ds", tB)
                p.wait("v", "t")
                for r in range(8):
                    sl = slice(r * T1, (r + 1) * T1)
                    p.inc(nc.vector.tensor_scalar(ym[:, sl], yB[:, sl], m_B[:, 0:1], None,
                                                  ALU.mult), "v")
                    p.inc(nc.vector.scalar_tensor_tensor(ym[:, sl], yA[:, sl], m_A[:, 0:1],
                                                         ym[:, sl], ALU.mult, ALU.add), "v")
                vtok_ym = p.cnt["v"]
                p.wait("t", "v", vtok_ym)
                p.wait("t", "dg")
                for m in range(8):
                    pbank = (ps_a, ps_b, ps_c, ps_d)[m % 4]
                    p.wait("t", "v", tok_1b[m % 4])
                    for kk in range(8):
                        p.inc(nc.tensor.matmul(pbank[:, 0:T1],
                                               wout_b[kk][:, 128 * m:128 * (m + 1)],
                                               ym[:, kk * T1:(kk + 1) * T1],
                                               start=(kk == 0), stop=(kk == 7)), "t")
                    p.wait("v", "t")
                    p.wait("v", "ds", tX)
                    p.inc(nc.vector.scalar_tensor_tensor(
                        xres[m][:, T1 * ci:T1 * (ci + 1)], pbank[:, 0:T1],
                        bout_t[:, m:m + 1], xm[:, m * T1:(m + 1) * T1],
                        ALU.add, ALU.add), "v")
                    tok_1b[m % 4] = p.cnt["v"]
        p.barrier()
        ph_w.close(); open_stacks.remove(ph_w)

        def emit_debug_and_return():
            for r in range(8):
                p.wait("q", "v"); p.wait("q", "s"); p.wait("q", "t")
                p.dma("q", yout[128 * r:128 * (r + 1), :], xres[r][:])
            p.wait("q", "ds"); p.wait("q", "da"); p.wait("q", "dg")
            for st_ in reversed(open_stacks):
                st_.close()

        if debug_phase in (1, 11):
            emit_debug_and_return()
            return nc

        def load_sign(tmp, sgn_b):
            for r in range(8):
                t = p.dma("q", tmp[:], sgnT[128 * r:128 * (r + 1), :])
                p.wait("s", "ds", t)
                p.inc(nc.scalar.activation(sgn_b[r][:], tmp[:], AF.Sign), "s")
                p.wait("q", "s")

        def ln_apply_bf16(src_slices, T, dst, dst_T=None):
            dst_T = dst_T or T
            p.wait("v", "t")
            for r in range(8):
                p.inc(nc.vector.tensor_mul(stats.sqb[0][:, 0:T], src_slices[r],
                                           ps_r[:, 0:T]), "v")
                p.inc(nc.vector.tensor_add(dst[:, r * dst_T:r * dst_T + T],
                                           stats.sqb[0][:, 0:T], ps_m[:, 0:T]), "v")
            stats_mark()

        def sign_ffn(src_tiles, ci, T, add_into, sgn_b, lnb, out_dma=None):
            c0 = T * ci
            src = [src_tiles[r][:, c0:c0 + T] for r in range(8)]
            stats_guard()
            stats(src, T, rms_only=False)
            ln_apply_bf16(src, T, lnb, dst_T=TA)
            vtok = p.cnt["v"]
            p.wait("t", "v", vtok)
            for m in range(8):
                pbank = (ps_a, ps_b, ps_c, ps_d)[m % 4]
                p.wait("t", "v", sign_ffn.tok[m % 4])
                for kk in range(8):
                    p.inc(nc.tensor.matmul(pbank[:, 0:T],
                                           sgn_b[kk][:, 128 * m:128 * (m + 1)],
                                           lnb[:, kk * TA:kk * TA + T],
                                           start=(kk == 0), stop=(kk == 7)), "t")
                p.wait("v", "t")
                p.inc(nc.vector.tensor_add(add_into[m][:, c0:c0 + T],
                                           add_into[m][:, c0:c0 + T], pbank[:, 0:T]), "v")
                sign_ffn.tok[m % 4] = p.cnt["v"]
                if out_dma is not None:
                    p.wait("q", "v")
                    p.dma("q", out_dma[128 * m:128 * (m + 1), c0:c0 + T],
                          add_into[m][:, c0:c0 + T])

        sign_ffn.tok = [0, 0, 0, 0]
        ph2s = ExitStack(); open_stacks.append(ph2s)
        sgn_b = [ph2s.enter_context(nc.sbuf_tensor(f"sgn_b{r}", [128, D], BF16))
                 for r in range(8)]
        lnb = ph2s.enter_context(nc.sbuf_tensor("lnb", [128, 8 * TA], BF16))
        load_sign(tmpw, sgn_b)
        p.barrier()
        for ci in range(NCA):
            sign_ffn(xres, ci, TA, xres, sgn_b, lnb)
        p.barrier()
        ph2s.close(); open_stacks.remove(ph2s)
        if debug_phase == 2:
            emit_debug_and_return()
            return nc

        ph_ffb = ExitStack(); open_stacks.append(ph_ffb)
        ffb = ph_ffb.enter_context(nc.sbuf_tensor("ffb", [128, 4 * D], BF16))
        for r in range(8):
            p.wait("g", "ds")
            t = p.dma("g", ffb[:], ff1T[128 * r:128 * (r + 1), :])
            p.wait("q", "dg", t)
            p.dma("q", ff1_bf.ap()[128 * r:128 * (r + 1), :], ffb[:])
        for r in range(8):
            p.wait("g", "ds")
            t = p.dma("g", r8(ffb[:, 0:4 * D], r=4),
                      ff2T[512 * r:512 * (r + 1), :].rearrange("(r p) t -> p r t", p=128))
            p.wait("q", "dg", t)
            p.dma("q", ff2_bf.ap()[512 * r:512 * (r + 1), :]
                  .rearrange("(r p) t -> p r t", p=128), r8(ffb[:, 0:4 * D], r=4))
        p.wait("q", "dg"); p.wait("q", "ds")
        p.barrier()
        ph_ffb.close(); open_stacks.remove(ph_ffb)

        for ci in range(NCA):
            c0 = TA * ci
            src = [xres[r][:, c0:c0 + TA] for r in range(8)]
            stats_guard()
            stats(src, TA, rms_only=False)
            for r in range(8):
                p.dma("q", x2_save.ap()[128 * r:128 * (r + 1), c0:c0 + TA], src[r])
            p.wait("v", "t")
            p.wait("v", "ds")
            for r in range(8):
                p.inc(nc.vector.tensor_mul(stats.sqb[0][:], src[r], ps_r[:, 0:TA]), "v")
                p.inc(nc.vector.tensor_add(src[r], stats.sqb[0][:], ps_m[:, 0:TA]), "v")
            stats_mark()
        p.barrier()
        if debug_phase == 25:
            emit_debug_and_return()
            return nc

        ph3a = ExitStack()
        with ph3a:
            s3 = lambda name, shape, dt=F32: ph3a.enter_context(nc.sbuf_tensor(name, shape, dt))
            pw_b = [s3(f"pw_b{r}", [128, D], BF16) for r in range(8)]
            for r in range(8):
                p.dma("g", pw_b[r][:], pwT[128 * r:128 * (r + 1), :])
            xnf = s3("xnf", [128, 8 * (TA + 2)])
            cvb = s3("cvb", [128, 8 * TA], BF16)
            halo_sb = s3("halo_sb", [128, 16])
            cvt = s3("cvt", [128, TA])
            tok_pw = [0, 0, 0, 0]

            stats_guard()
            stats([xres[r][:, HL - 2:HL] for r in range(8)], 2, rms_only=True)
            p.wait("v", "t")
            for r in range(8):
                p.inc(nc.vector.scalar_tensor_tensor(halo_sb[:, 2 * r:2 * r + 2],
                                                     xres[r][:, HL - 2:HL],
                                                     cgam[:, r:r + 1], ps_r[:, 0:2],
                                                     ALU.mult, ALU.mult), "v")
            stats_mark()
            p.wait("q", "v")
            p.dma("q", halo_in.ap(), halo_sb[:])
            p.wait("g", "ds")
            p.inc(nc.gpsimd.collective_compute(
                "AllGather", ALU.bypass, replica_groups=GROUPS,
                ins=[halo_in.ap()], outs=[halo_out.ap()]), "cc")
            p.wait("q", "cc")
            tH = p.dma("q", halo_sb[:], halo_out.ap()[0:128, :])
            p.wait("v", "ds", tH)
            for r in range(8):
                base = r * (TA + 2)
                p.inc(nc.vector.tensor_scalar(xnf[:, base:base + 2],
                                              halo_sb[:, 2 * r:2 * r + 2],
                                              m_R[:, 0:1], None, ALU.mult), "v")

            convb_bf = s3("convb_bf", [128, 8], BF16)
            p.inc(nc.vector.tensor_copy(convb_bf[:], convb_t[:]), "v")
            p.wait("t", "dg")
            p.wait("t", "v")
            for m in range(8):
                for kk in range(8):
                    p.inc(nc.tensor.matmul(ps_d[:, m:m + 1],
                                           pw_b[kk][:, 128 * m:128 * (m + 1)],
                                           convb_bf[:, kk:kk + 1],
                                           start=(kk == 0), stop=(kk == 7)), "t")
            p.wait("v", "t")
            p.inc(nc.vector.tensor_add(beff_t[:], ps_d[:, 0:8], pwb_t[:]), "v")
            tok_pw[3] = p.cnt["v"]

            for ci in range(NCA):
                c0 = TA * ci
                stats_guard()
                stats([xres[r][:, c0:c0 + TA] for r in range(8)], TA, rms_only=True)
                p.wait("v", "t")
                for r in range(8):
                    base = r * (TA + 2)
                    p.inc(nc.vector.scalar_tensor_tensor(
                        xnf[:, base + 2:base + 2 + TA], xres[r][:, c0:c0 + TA],
                        cgam[:, r:r + 1], ps_r[:, 0:TA], ALU.mult, ALU.mult), "v")
                stats_mark()
                for r in range(8):
                    base = r * (TA + 2)
                    p.inc(nc.vector.tensor_scalar(cvt[:], xnf[:, base:base + TA],
                                                  convw_t[:, 3 * r:3 * r + 1], None,
                                                  ALU.mult), "v")
                    p.inc(nc.vector.scalar_tensor_tensor(cvt[:], xnf[:, base + 1:base + 1 + TA],
                                                         convw_t[:, 3 * r + 1:3 * r + 2],
                                                         cvt[:], ALU.mult, ALU.add), "v")
                    p.inc(nc.vector.scalar_tensor_tensor(cvb[:, r * TA:(r + 1) * TA],
                                                         xnf[:, base + 2:base + 2 + TA],
                                                         convw_t[:, 3 * r + 2:3 * r + 3],
                                                         cvt[:], ALU.mult, ALU.add), "v")
                    p.inc(nc.vector.tensor_copy(xnf[:, base:base + 2],
                                                xnf[:, base + TA:base + TA + 2]), "v")
                vtok_cv = p.cnt["v"]
                p.wait("t", "v", vtok_cv)
                for m in range(8):
                    pbank = (ps_a, ps_b, ps_c, ps_d)[m % 4]
                    p.wait("t", "v", tok_pw[m % 4])
                    for kk in range(8):
                        p.inc(nc.tensor.matmul(pbank[:, 0:TA],
                                               pw_b[kk][:, 128 * m:128 * (m + 1)],
                                               cvb[:, kk * TA:(kk + 1) * TA],
                                               start=(kk == 0), stop=(kk == 7)), "t")
                    p.wait("v", "t")
                    p.inc(nc.vector.scalar_tensor_tensor(
                        xres[m][:, c0:c0 + TA], pbank[:, 0:TA], beff_t[:, m:m + 1],
                        xres[m][:, c0:c0 + TA], ALU.add, ALU.add), "v")
                    tok_pw[m % 4] = p.cnt["v"]
        p.barrier()
        if debug_phase == 3:
            emit_debug_and_return()
            return nc

        ph3b = ExitStack()
        with ph3b:
            s3b = lambda name, shape, dt=F32: ph3b.enter_context(nc.sbuf_tensor(name, shape, dt))
            gru_bt = [s3b(f"gru_bt{r}", [128, 2 * D], BF16) for r in range(8)]
            for r in range(8):
                p.dma("g", gru_bt[r][:], gruT[128 * r:128 * (r + 1), :])
            xg = s3b("xg", [128, 8 * TB], BF16)
            cbuf = s3b("cbuf", [128, 8 * TB])
            vbuf = s3b("vbuf", [128, 8 * TB])
            sgb = [s3b(f"sgb{i}", [128, TB]) for i in range(4)]
            hloc = s3b("hloc", [128, TB])
            zbuf = sgb[3]
            tok_gru_s = [0, 0, 0, 0]; tok_gru_v = [0, 0, 0, 0]
            pbf = s3b("pbf", [128, 8 * TB], BF16)

            tok_pd = 0
            for ci in range(NCB):
                c0 = TB * ci
                stats_guard()
                stats([xres[r][:, c0:c0 + TB] for r in range(8)], TB, rms_only=True)
                p.wait("v", "t")
                p.wait("v", "g")
                for r in range(8):
                    p.inc(nc.vector.scalar_tensor_tensor(
                        xg[:, r * TB:(r + 1) * TB], xres[r][:, c0:c0 + TB],
                        ggam[:, r:r + 1], ps_r[:, 0:TB], ALU.mult, ALU.mult), "v")
                stats_mark()
                vtok_xg = p.cnt["v"]
                p.wait("t", "v", vtok_xg)
                p.wait("t", "dg")
                for m in range(16):
                    pbank = (ps_a, ps_b, ps_c, ps_d)[m % 4]
                    p.wait("t", "s", tok_gru_s[m % 4])
                    p.wait("t", "v", tok_gru_v[m % 4])
                    for kk in range(8):
                        p.inc(nc.tensor.matmul(pbank[:, 0:TB],
                                               gru_bt[kk][:, 128 * m:128 * (m + 1)],
                                               xg[:, kk * TB:(kk + 1) * TB],
                                               start=(kk == 0), stop=(kk == 7)), "t")
                    r = m % 8
                    sl = slice(r * TB, (r + 1) * TB)
                    if m < 8:
                        p.wait("s", "t")
                        p.inc(nc.scalar.activation(sgb[m % 4][:], pbank[:, 0:TB],
                                                   AF.Sigmoid), "s")
                        p.wait("v", "s")
                        p.inc(nc.vector.scalar_tensor_tensor(vbuf[:, sl], pbank[:, 0:TB],
                                                             0.5, sgb[m % 4][:],
                                                             ALU.add, ALU.max), "v")
                        tok_gru_v[m % 4] = p.cnt["v"]
                        tok_gru_s[m % 4] = p.cnt["s"]
                    else:
                        p.wait("s", "t")
                        p.inc(nc.scalar.activation(cbuf[:, sl], pbank[:, 0:TB],
                                                   AF.Sigmoid, scale=-1.0), "s")
                        p.inc(nc.scalar.activation(sgb[m % 4][:], pbank[:, 0:TB],
                                                   AF.Sigmoid), "s")
                        tok_gru_s[m % 4] = p.cnt["s"]
                        p.wait("v", "s")
                        p.inc(nc.vector.tensor_mul(vbuf[:, sl], vbuf[:, sl],
                                                   sgb[m % 4][:]), "v")
                        tok_gru_v[m % 4] = p.cnt["v"]
                p.wait("v", "s")
                p.inc(nc.vector.memset(zbuf[:], 0.0), "v")
                for r in range(8):
                    sl = slice(r * TB, (r + 1) * TB)
                    p.wait("v", "s")
                    p.inc(nc.vector.tensor_tensor_scan(
                        hloc[:], cbuf[:, sl], vbuf[:, sl],
                        gru_state[:, r:r + 1], ALU.mult, ALU.add), "v")
                    p.wait("s", "v")
                    p.inc(nc.scalar.copy(gru_state[:, r:r + 1], hloc[:, TB - 1:TB]), "s")
                    p.inc(nc.vector.tensor_add(xres[r][:, c0:c0 + TB],
                                               xres[r][:, c0:c0 + TB], hloc[:]), "v")
                    if tok_pd:
                        p.wait("v", "ds", tok_pd)
                    p.inc(nc.vector.tensor_tensor_scan(
                        pbf[:, sl], cbuf[:, sl], zbuf[:],
                        p_state[:, r:r + 1], ALU.mult, ALU.add), "v")
                    p.wait("s", "v")
                    p.inc(nc.scalar.copy(p_state[:, r:r + 1],
                                         pbf[:, (r + 1) * TB - 1:(r + 1) * TB]), "s")
                p.wait("q", "v")
                for r in range(8):
                    tok_pd = p.dma("q", p_save.ap()[128 * r:128 * (r + 1), c0:c0 + TB],
                                   pbf[:, r * TB:(r + 1) * TB])
            p.wait("q", "s")
            p.dma("q", gst_in.ap(), gru_state[:])
            p.wait("g", "ds")
            p.inc(nc.gpsimd.collective_compute(
                "AllGather", ALU.bypass, replica_groups=GROUPS,
                ins=[gst_in.ap()], outs=[gst_out.ap()]), "cc")
            p.wait("q", "cc")
            tG = p.dma("q", gst_sb[:], gst_out.ap()[0:128, :])
            p.wait("v", "ds", tG)
            p.inc(nc.vector.tensor_scalar(gst_sb[:], gst_sb[:], m_R[:, 0:1], None,
                                          ALU.mult), "v")
        p.barrier()
        ph3corr = ExitStack()
        with ph3corr:
            pcor = ph3corr.enter_context(nc.sbuf_tensor("pcor", [128, 8 * TA], BF16))
            for ci in range(NCA):
                c0 = TA * ci
                t = p.dma("q", r8(pcor[:]), d8(p_save.ap()[:, c0:c0 + TA]))
                p.wait("v", "ds", t)
                for r in range(8):
                    p.inc(nc.vector.scalar_tensor_tensor(
                        xres[r][:, c0:c0 + TA], pcor[:, r * TA:(r + 1) * TA],
                        gst_sb[:, r:r + 1], xres[r][:, c0:c0 + TA],
                        ALU.mult, ALU.add), "v")
                p.wait("q", "v")
                for r in range(8):
                    p.dma("q", z_save.ap()[128 * r:128 * (r + 1), c0:c0 + TA],
                          xres[r][:, c0:c0 + TA])
        p.barrier()
        if debug_phase == 35:
            emit_debug_and_return()
            return nc
        ph_res.close(); open_stacks.remove(ph_res)

        ph3c = ExitStack()
        with ph3c:
            s3c = lambda name, shape, dt=F32: ph3c.enter_context(nc.sbuf_tensor(name, shape, dt))
            w1q = [s3c(f"w1q{i}", [128, 8 * D], BF16) for i in range(3)]
            w2q = [s3c(f"w2q{i}", [128, 32 * 256], BF16) for i in range(3)]
            h1b = s3c("h1b", [128, 32 * TA], BF16)
            ln3 = [s3c(f"ln3_{i}", [128, 8 * TA], BF16) for i in range(2)]
            zc = [s3c(f"zc{i}", [128, 8 * TA]) for i in range(2)]

            def load_w1q(qi):
                return p.dma("g", r8(w1q[qi % 3][:]),
                             d8(ff1_bf.ap()[:, 1024 * (qi % 4):1024 * (qi % 4 + 1)]))

            def load_w2q(qi):
                return p.dma("s", r8(w2q[qi % 3][:], r=32),
                             d8(ff2_bf.ap()[:, 256 * (qi % 4):256 * (qi % 4 + 1)]))

            tz = [0, 0]
            tw1 = {}
            tw2 = {}
            tok_w1use = {-3: 0, -2: 0, -1: 0}
            tok_w2use = {-3: 0, -2: 0, -1: 0}
            tok_ff1 = [0, 0]
            tok_ff2 = [0, 0]
            tok_zuse = [0, 0]
            vtok_ln3 = [0, 0]

            def stats3c(ci):
                cur = ci % 2
                Z = zc[cur]
                Zs = [Z[:, r * TA:(r + 1) * TA] for r in range(8)]
                stats_guard()
                p.wait("t", "ds", tz[cur]); p.wait("s", "ds", tz[cur])
                stats(Zs, TA, rms_only=True)
                p.wait("v", "t")
                for r in range(8):
                    p.inc(nc.vector.scalar_tensor_tensor(
                        ln3[cur][:, r * TA:(r + 1) * TA], Zs[r],
                        fgam[:, r:r + 1], ps_r[:, 0:TA], ALU.mult, ALU.mult), "v")
                stats_mark()
                vtok_ln3[cur] = p.cnt["v"]

            tz[0] = p.dma("q", r8(zc[0][:]), d8(z_save.ap()[:, 0:TA]))
            tz[1] = p.dma("q", r8(zc[1][:]), d8(z_save.ap()[:, TA:2 * TA]))
            tw1[0] = load_w1q(0); tw1[1] = load_w1q(1)
            tw2[0] = load_w2q(0); tw2[1] = load_w2q(1)
            stats3c(0)
            for ci in range(NCA):
                cur = ci % 2
                Z = zc[cur]
                l3 = ln3[cur]
                for qi4 in range(4):
                    qi = 4 * ci + qi4
                    p.wait("t", "dg", tw1[qi])
                    p.wait("t", "v", vtok_ln3[cur])
                    for mm in range(8):
                        m = qi4 * 8 + mm
                        pbank = ps_a if mm % 2 == 0 else ps_b
                        p.wait("t", "s", tok_ff1[mm % 2])
                        for kk in range(8):
                            p.inc(nc.tensor.matmul(
                                pbank[:, 0:TA],
                                w1q[qi % 3][:, kk * 1024 + 128 * mm:kk * 1024 + 128 * (mm + 1)],
                                l3[:, kk * TA:(kk + 1) * TA],
                                start=(kk == 0), stop=(kk == 7)), "t")
                        p.wait("s", "t")
                        p.inc(nc.scalar.activation(h1b[:, m * TA:(m + 1) * TA],
                                                   pbank[:, 0:TA], AF.Gelu,
                                                   bias=b1_t[:, m:m + 1]), "s")
                        tok_ff1[mm % 2] = p.cnt["s"]
                    tok_w1use[qi] = p.cnt["t"]
                    if qi + 2 < 4 * NCA:
                        p.wait("g", "t", tok_w1use.get(qi - 1, 0))
                        tw1[qi + 2] = load_w1q(qi + 2)
                stok_h1 = p.cnt["s"]
                if ci + 1 < NCA:
                    stats3c(ci + 1)
                for qi4 in range(4):
                    qi = 4 * ci + qi4
                    p.wait("t", "da", tw2[qi])
                    p.wait("t", "s", stok_h1)
                    for mm in range(2):
                        m = qi4 * 2 + mm
                        pbank = ps_c if mm % 2 == 0 else ps_d
                        p.wait("t", "v", tok_ff2[mm % 2])
                        for kk in range(32):
                            p.inc(nc.tensor.matmul(
                                pbank[:, 0:TA],
                                w2q[qi % 3][:, kk * 256 + 128 * mm:kk * 256 + 128 * (mm + 1)],
                                h1b[:, kk * TA:(kk + 1) * TA],
                                start=(kk == 0), stop=(kk == 31)), "t")
                        p.wait("v", "t")
                        p.inc(nc.vector.scalar_tensor_tensor(
                            Z[:, m * TA:(m + 1) * TA], pbank[:, 0:TA], b2_t[:, m:m + 1],
                            Z[:, m * TA:(m + 1) * TA], ALU.add, ALU.add), "v")
                        tok_ff2[mm % 2] = p.cnt["v"]
                    tok_w2use[qi] = p.cnt["t"]
                    if qi + 2 < 4 * NCA:
                        p.wait("s", "t", tok_w2use.get(qi - 1, 0))
                        tw2[qi + 2] = load_w2q(qi + 2)
                tok_zuse[cur] = p.cnt["v"]
                p.wait("q", "v")
                for r in range(8):
                    p.dma("q", z_save.ap()[128 * r:128 * (r + 1), TA * ci:TA * (ci + 1)],
                          Z[:, r * TA:(r + 1) * TA])
                if ci + 2 < NCA:
                    tz[cur] = p.dma("q", r8(zc[cur][:]),
                                    d8(z_save.ap()[:, TA * (ci + 2):TA * (ci + 3)]))
        p.barrier()

        ph4 = ExitStack()
        with ph4:
            s4 = lambda name, shape, dt=F32: ph4.enter_context(nc.sbuf_tensor(name, shape, dt))
            sgn_b = [s4(f"sgn4_{r}", [128, D], BF16) for r in range(8)]
            lnb = s4("lnb4", [128, 8 * TA], BF16)
            load_sign(tmpw, sgn_b)
            p.barrier()
            x3c = [s4(f"x3c{i}", [128, 8 * TA]) for i in range(2)]
            zc4 = s4("zc4", [128, 8 * TA])
            tok_p4 = [0, 0, 0, 0]
            for ci in range(NCA):
                cur = ci % 2
                c0 = TA * ci
                p.wait("q", "v"); p.wait("q", "s"); p.wait("q", "t")
                tS = p.dma("q", r8(x3c[cur][:]), d8(x2_save.ap()[:, c0:c0 + TA]))
                tZ = p.dma("q", r8(zc4[:]), d8(z_save.ap()[:, c0:c0 + TA]))
                p.wait("v", "ds", tZ)
                for r in range(8):
                    sl = slice(r * TA, (r + 1) * TA)
                    p.inc(nc.vector.tensor_add(x3c[cur][:, sl], x3c[cur][:, sl],
                                               zc4[:, sl]), "v")
                vtok_x3 = p.cnt["v"]
                X3 = [x3c[cur][:, r * TA:(r + 1) * TA] for r in range(8)]
                if debug_phase == 4:
                    p.wait("q", "v")
                    for r in range(8):
                        p.dma("q", yout[128 * r:128 * (r + 1), c0:c0 + TA],
                              zc4[:, r * TA:(r + 1) * TA])
                    continue
                stats_guard()
                p.wait("t", "v", vtok_x3)
                stats(X3, TA, rms_only=False)
                ln_apply_bf16(X3, TA, lnb, dst_T=TA)
                p.wait("t", "v", p.cnt["v"])
                for m in range(8):
                    pbank = (ps_a, ps_b, ps_c, ps_d)[m % 4]
                    p.wait("t", "v", tok_p4[m % 4])
                    for kk in range(8):
                        p.inc(nc.tensor.matmul(pbank[:, 0:TA],
                                               sgn_b[kk][:, 128 * m:128 * (m + 1)],
                                               lnb[:, kk * TA:kk * TA + TA],
                                               start=(kk == 0), stop=(kk == 7)), "t")
                    p.wait("v", "t")
                    p.inc(nc.vector.tensor_add(x3c[cur][:, m * TA:(m + 1) * TA],
                                               x3c[cur][:, m * TA:(m + 1) * TA],
                                               pbank[:, 0:TA]), "v")
                    tok_p4[m % 4] = p.cnt["v"]
                    p.wait("q", "v")
                    p.dma("q", yout[128 * m:128 * (m + 1), c0:c0 + TA],
                          x3c[cur][:, m * TA:(m + 1) * TA])

        p.wait("q", "ds"); p.wait("q", "da"); p.wait("q", "dg")
    return nc


_NC_CACHE = {}


def _get_nc():
    key = DEBUG_PHASE
    if key not in _NC_CACHE:
        _NC_CACHE[key] = build_program(DEBUG_PHASE)
    return _NC_CACHE[key]


def _prep_core_inputs(c, full):
    b, q = c // 2, c % 2
    x = full["L0_attn_a_x"]
    xb = np.ascontiguousarray(x[b].T)
    perm = np.concatenate([np.arange(512 * q, 512 * q + 512),
                           np.arange(512 * (1 - q), 512 * (1 - q) + 512)])
    xT = np.ascontiguousarray(xb[perm])
    x_mine = np.ascontiguousarray(xb[:, HL * q:HL * (q + 1)])
    A_log = np.asarray(full["L0_attn_a_A_log"], np.float32)
    heads = np.arange(8 * q, 8 * q + 8)
    a_log_arr = np.zeros((128, NPAIR * NS), np.float32)
    for k in range(NPAIR):
        hA, hB = heads[2 * k], heads[2 * k + 1]
        a_log_arr[0:64, k * NS:(k + 1) * NS] = A_log[hA][None, :]
        a_log_arr[64:128, k * NS:(k + 1) * NS] = A_log[hB][None, :]
    bdt = np.asarray(full["L0_attn_a_bdt"], np.float32)
    bdt_pair = np.concatenate([bdt, bdt]).reshape(128, 1)

    col8 = lambda v: np.ascontiguousarray(np.asarray(v, np.float32).reshape(8, 128).T)
    col32 = lambda v: np.ascontiguousarray(np.asarray(v, np.float32).reshape(32, 128).T)
    dww = np.asarray(full["L0_attn_b_conv_dw_w"], np.float32).reshape(1024, 3)
    convw = np.zeros((128, 24), np.float32)
    for r in range(8):
        convw[:, 3 * r:3 * r + 3] = dww[128 * r:128 * (r + 1), :]

    ct = lambda a: np.ascontiguousarray(np.asarray(a, np.float32).T)
    return dict(
        xT=xT, x_mine=x_mine,
        wxT=ct(full["L0_attn_a_Wx"]),
        wx_nat=np.ascontiguousarray(np.asarray(full["L0_attn_a_Wx"], np.float32)),
        wdtT=ct(full["L0_attn_a_Wdt"]),
        a_log_arr=a_log_arr, bdt_pair=bdt_pair,
        woutT=ct(full["L0_attn_a_Wout"]),
        bout_col=col8(full["L0_attn_a_bout"]),
        sgnT=ct(full["L0_ffn_weight"]),
        convw=convw,
        convb_col=col8(full["L0_attn_b_conv_dw_b"]),
        pwb_col=col8(full["L0_attn_b_conv_pw_b"]),
        conv_g=col8(full["L0_attn_b_conv_norm_g"]),
        gru_g=col8(full["L0_attn_b_gru_norm_g"]),
        ff_g=col8(full["L0_attn_b_ff_norm_g"]),
        pwT=ct(full["L0_attn_b_conv_pw_w"]),
        gruT=ct(full["L0_attn_b_gru_w"]),
        ff1T=ct(full["L0_attn_b_ff_w1"]),
        ff2T=ct(full["L0_attn_b_ff_w2"]),
        b1_col=col32(full["L0_attn_b_ff_b1"]),
        b2_col=col8(full["L0_attn_b_ff_b2"]),
        maskA=np.full((128, 1), 1.0 if q == 0 else 0.0, np.float32),
        maskB=np.full((128, 1), 0.0 if q == 0 else 1.0, np.float32),
        mrecv=np.full((128, 1), 1.0 if q == 1 else 0.0, np.float32),
    )


def kernel(**inputs):
    nc = _get_nc()
    in_maps = [_prep_core_inputs(c, inputs) for c in range(NCORES)]
    res = run_bass_kernel_spmd(nc, in_maps, core_ids=list(range(NCORES)))
    out = np.empty((4, L, D), np.float32)
    for c in range(NCORES):
        b, q = c // 2, c % 2
        out[b, HL * q:HL * (q + 1), :] = res.results[c]["yout"].T
    return out
